# version 26 (final)
# baseline (speedup 1.0000x reference)
import numpy as np
from contextlib import ExitStack

import concourse.bass as bass
import concourse.mybir as mybir
from concourse.bass_utils import run_bass_kernel_spmd

F32 = mybir.dt.float32
BF16 = mybir.dt.bfloat16
FP16 = mybir.dt.float16
AF = mybir.ActivationFunctionType
ALU = mybir.AluOpType
AX = mybir.AxisListType

D = 1024
E = 2048
S = 16
TT = 512
NCH_D = 8
NCH_E = 16
XMC = 516
NEXP = 20
NGRP = 5
EPS = 1e-5
RSCALE = 2.5
BIG = 1.0e30

ENGINES = ("pe", "act", "dve", "pool", "sp")
BLOCK_ATTR = {"pe": "tensor", "act": "scalar", "dve": "vector", "pool": "gpsimd", "sp": "sync"}


class Buf:
    __slots__ = ("w", "r")

    def __init__(self):
        self.w = []
        self.r = []


class Plan:

    COARSE = ("spc", "c_misc", "wb", "xln0", "xln1", "st", "dbgq")

    def __init__(self):
        self.ops = []

    def op(self, eng, fn, reads=(), writes=(), deps=(), dma=False, sem=None):
        dl = [d for d in deps if d is not None]
        for b in reads:
            dl += b.w
        for b in writes:
            dl += b.w + b.r
        i = len(self.ops)
        if eng == "pe":
            dl = [d for d in dl if self.ops[d]["eng"] != "pe"]
        dl = sorted(set(dl))
        if dma:
            assert sem is not None
        self.ops.append({"eng": eng, "fn": fn, "deps": dl, "dma": dma,
                         "sem": sem if dma else eng, "inc": False, "val": None})
        for b in reads:
            b.r.append(i)
        for b in writes:
            b.w = [i]
            b.r = []
        return i

    def finalize_and_emit(self, nc, make_sem):
        for o in self.ops:
            if o["dma"]:
                o["inc"] = True
            for d in o["deps"]:
                self.ops[d]["inc"] = True
        semkeys = sorted({o["sem"] for o in self.ops})
        cnt = {k: 0 for k in semkeys}
        for o in self.ops:
            if o["inc"]:
                cnt[o["sem"]] += 16 if o["dma"] else 1
                o["val"] = cnt[o["sem"]]
        run = {k: 0 for k in semkeys}
        last = {e: {k: 0 for k in semkeys} for e in ENGINES}
        for o in self.ops:
            need = {}
            for d in o["deps"]:
                do = self.ops[d]
                k = do["sem"]
                v = run[k] if k in self.COARSE else do["val"]
                if v is None:
                    continue
                if v > need.get(k, 0):
                    need[k] = v
            me = o["eng"]
            w = []
            for k, v in need.items():
                if v > last[me][k]:
                    w.append((k, v))
                    last[me][k] = v
            o["waits"] = w
            if o["inc"]:
                run[o["sem"]] += 16 if o["dma"] else 1

        sems = {k: make_sem(k) for k in semkeys}
        per = {e: [o for o in self.ops if o["eng"] == e] for e in ENGINES}

        def runner(eng_name):
            def body(eng):
                for o in per[eng_name]:
                    for (k, v) in o["waits"]:
                        eng.wait_ge(sems[k], v)
                    instr = o["fn"](eng)
                    if o["inc"]:
                        instr.then_inc(sems[o["sem"]], 16 if o["dma"] else 1)
            return body

        with nc.Block() as block:
            for e in ENGINES:
                getattr(block, BLOCK_ATTR[e])(runner(e))
        return {e: len(per[e]) for e in ENGINES}


def build_program(nt=8, debug=False):
    nc = bass.Bass("TRN2", target_bir_lowering=False, debug=False)

    xT = nc.dram_tensor("xT", [D, 4096], F32, kind="ExternalInput").ap()
    in_wt = nc.dram_tensor("in_wt", [32, 128, 1024], F32, kind="ExternalInput").ap()
    dt_wt = nc.dram_tensor("dt_wt", [16, 128, 2048], F32, kind="ExternalInput").ap()
    out_wt = nc.dram_tensor("out_wt", [8, 128, 2048], F32, kind="ExternalInput").ap()
    up_wt = nc.dram_tensor("up_wt", [NEXP, 2, 128, 2048], F32, kind="ExternalInput").ap()
    dn_wt = nc.dram_tensor("dn_wt", [8, NGRP, 128, 2048], F32, kind="ExternalInput").ap()
    bcd_w = nc.dram_tensor("bcd_w", [128, 1280], F32, kind="ExternalInput").ap()
    gate_wl = nc.dram_tensor("gate_wl", [128, 128], F32, kind="ExternalInput").ap()
    conv_wl = nc.dram_tensor("conv_wl", [128, 48], F32, kind="ExternalInput").ap()
    conv_bl = nc.dram_tensor("conv_bl", [128, 16], F32, kind="ExternalInput").ap()
    in_bl = nc.dram_tensor("in_bl", [128, 32], F32, kind="ExternalInput").ap()
    dt_bl = nc.dram_tensor("dt_bl", [128, 16], F32, kind="ExternalInput").ap()
    out_bl = nc.dram_tensor("out_bl", [128, 8], F32, kind="ExternalInput").ap()
    bcd_b = nc.dram_tensor("bcd_b", [16, 3], F32, kind="ExternalInput").ap()
    bias_e_tm = nc.dram_tensor("bias_e_tm", [1, 64], F32, kind="ExternalInput").ap()
    eye = nc.dram_tensor("eye", [128, 128], F32, kind="ExternalInput").ap()

    outT = nc.dram_tensor("outT", [D, 4096], F32, kind="ExternalOutput").ap()

    in_bf = nc.dram_tensor("in_bf", [32, 128, 1024], FP16, kind="Internal").ap()
    dt_bf = nc.dram_tensor("dt_bf", [16, 128, 2048], FP16, kind="Internal").ap()
    out_bf = nc.dram_tensor("out_bf", [8, 128, 2048], FP16, kind="Internal").ap()
    up_bf = nc.dram_tensor("up_bf", [NEXP, 2, 128, 2048], FP16, kind="Internal").ap()
    dn_bf = nc.dram_tensor("dn_bf", [8, NGRP, 128, 2048], FP16, kind="Internal").ap()
    bcd_bfd = nc.dram_tensor("bcd_bfd", [128, 1280], FP16, kind="Internal").ap()
    gate_bfd = nc.dram_tensor("gate_bfd", [128, 128], FP16, kind="Internal").ap()

    dbg = {}
    if debug:
        def dbgt(name, shape):
            dbg[name] = nc.dram_tensor("dbg_" + name, shape, F32, kind="ExternalOutput").ap()
        dbgt("h1", [128, NCH_D * TT])
        dbgt("xc", [128, NCH_E * XMC])
        dbgt("ld", [1, TT])
        dbgt("scan", [16, TT])
        dbgt("ao", [128, NCH_D * TT])
        dbgt("x2", [128, NCH_D * TT])
        dbgt("h2", [128, NCH_D * TT])
        dbgt("scores", [128, 64])
        dbgt("combine", [16, TT])
        dbgt("ffn", [128, NCH_D * TT])

    P = Plan()

    with ExitStack() as ctx:
        sb = lambda name, shape, dt=F32: ctx.enter_context(nc.sbuf_tensor(name, shape, dt))
        ps = lambda name: ctx.enter_context(nc.psum_tensor(name, [128, 512], F32))

        c_eye = sb("c_eye", [128, 128])
        c_ones_bf = sb("c_ones_bf", [1, 128], FP16)
        c_ones1x16 = sb("c_ones1x16", [1, 16])
        c_ones16 = sb("c_ones16", [16, 1])
        c_ones1x128 = sb("c_ones1x128", [1, 128])
        c_inv1024 = sb("c_inv1024", [128, 1], FP16)
        c_inv2048 = sb("c_inv2048", [128, 1], FP16)
        c_in_b = sb("c_in_b", [128, 32])
        c_dt_b = sb("c_dt_b", [128, 16])
        c_out_b = sb("c_out_b", [128, 8])
        c_conv_w = sb("c_conv_w", [128, 48])
        c_conv_b = sb("c_conv_b", [128, 16])
        c_bcd_b = sb("c_bcd_b", [16, 3])
        c_bcd_bf = sb("c_bcd_bf", [128, 1280], FP16)
        c_gate_bf = sb("c_gate_bf", [128, 128], FP16)
        c_bias_e_tm = sb("c_bias_e_tm", [128, 64])
        c_betm_row = sb("c_betm_row", [1, 64])
        c_eps = sb("c_eps", [128, 1])
        c_one = sb("c_one", [128, 1])

        xlnt = [sb("xln0", [128, NCH_D * TT], FP16), sb("xln1", [128, NCH_D * TT], FP16)]
        h1t = [sb("h1a", [128, NCH_D * TT], FP16), sb("h1b", [128, NCH_D * TT], FP16)]
        h2 = sb("h2", [128, NCH_D * TT], FP16)
        xmt = [sb("xm0", [128, NCH_E * XMC], FP16), sb("xm1", [128, NCH_E * XMC], FP16)]
        sg = sb("sg", [128, NCH_E * TT], FP16)
        dtb = sb("dtb", [128, 2 * TT], FP16)
        sq = sb("sq", [128, 2 * TT], FP16)
        etmp = sb("etmp", [128, 4 * TT])
        aob = sb("aob", [128, NCH_D * TT], FP16)
        x2 = sb("x2", [128, NCH_D * TT], FP16)
        zs = sb("zs", [128, 16 * TT], FP16)
        ostage = sb("ostage", [128, 2 * TT])
        ptmp16 = sb("ptmp16", [128, 2 * TT], FP16)

        u16 = sb("u16", [16, TT])
        tmp16 = sb("tmp16", [16, TT])
        ct16 = sb("ct16", [16, TT])
        scano = sb("scano", [16, TT])
        s0 = sb("s0", [16, 1])
        rowA = sb("rowA", [1, TT])
        rowB = sb("rowB", [1, TT])
        rowAb = sb("rowAb", [1, TT], FP16)
        rowBb = sb("rowBb", [1, TT], FP16)
        lg16 = sb("lg16", [16, TT])
        sc_tm = sb("sc_tm", [128, 64])
        scor_tm = sb("scor_tm", [128, 64])
        gs = sb("gs", [128, 16])
        gpair = sb("gpair", [128, 16])
        thr2 = sb("thr2", [128, 4])
        gmask = sb("gmask", [128, 16])
        maskd = sb("maskd", [128, 64])
        maskw = sb("maskw", [128, 64])
        cmpb = sb("cmpb", [128, 16])
        mxc = sb("mxc", [128, 4])
        selb = sb("selb", [128, 64])
        wtm = sb("wtm", [128, 64])
        den = sb("den", [128, 4])
        com_tm = sb("com_tm", [128, 64])
        com_em = sb("com_em", [16, TT], FP16)
        cwrow = sb("cwrow", [1, 5 * TT], FP16)

        rstd_sb = sb("rstd_sb", [128, TT], FP16)
        mrstd_sb = sb("mrstd_sb", [128, TT], FP16)
        ys_sb = sb("ys_sb", [128, TT], FP16)
        cw_sb = sb("cw_sb", [128, 2 * TT], FP16)
        etmp16 = sb("etmp16", [128, 2 * TT], FP16)
        rw = sb("rw", [128, 5 * 2048], FP16)
        rdn = sb("rdn", [128, 3 * 2048], FP16)

        pb = [ps(f"pb{i}") for i in range(8)]
        def make_sem(k):
            return ctx.enter_context(nc.semaphore("sem_" + k))

        B_pb = [Buf() for _ in range(8)]
        B_xln = [[Buf() for _ in range(NCH_D)] for _ in range(2)]
        B_h1 = [[Buf() for _ in range(NCH_D)] for _ in range(2)]
        B_h2 = [Buf() for _ in range(NCH_D)]
        B_xm = [[Buf() for _ in range(NCH_E)] for _ in range(2)]
        B_sg = [Buf() for _ in range(NCH_E)]
        B_dtb = [Buf() for _ in range(2)]
        B_sq = [Buf() for _ in range(2)]
        B_etmp = [Buf() for _ in range(4)]
        B_aob = [Buf() for _ in range(NCH_D)]
        B_x2 = [Buf() for _ in range(NCH_D)]
        B_zs = [Buf() for _ in range(16)]
        B_ostage = [Buf() for _ in range(2)]
        B_pt16 = [Buf(), Buf()]
        B_cwrow = [Buf() for _ in range(5)]
        B_rw = [Buf() for _ in range(5)]
        B_rdn = [Buf() for _ in range(3)]
        B_bc1 = Buf()
        B_bc2 = Buf()
        B_ys = Buf()
        B_cw = [Buf(), Buf()]
        B_et16 = [Buf(), Buf()]
        B_const = Buf()
        B_c_bcd = Buf()
        B_c_gate = Buf()
        B_c_betm = Buf()
        B_small = {nm: Buf() for nm in
                   ("u16", "tmp16", "ct16", "scano", "s0", "rowA", "rowB", "rowAb",
                    "rowBb", "lg16", "sc_tm", "scor_tm", "gs", "gpair", "thr2",
                    "gmask", "maskd", "maskw", "cmpb", "mxc", "selb", "wtm", "den",
                    "com_tm", "com_em")}
        B_cast = {"misc": Buf()}
        B_scr = {"in": [Buf() for _ in range(16)], "dt": [Buf() for _ in range(16)],
                 "out": [Buf() for _ in range(8)], "up": [[Buf(), Buf()] for _ in range(NEXP)],
                 "dn": [[Buf() for _ in range(NGRP)] for _ in range(8)]}

        def op(eng, fn, reads=(), writes=(), deps=(), dma=False, sem=None):
            return P.op(eng, fn, reads, writes, deps, dma, sem)

        def mm_group(bank, parts, out_sub=None):
            bb = B_pb[bank]
            war = list(bb.w) + list(bb.r)
            n = len(parts)
            last = None
            oa = pb[bank][:, :] if out_sub is None else out_sub
            for i, (lf, rf, reads) in enumerate(parts):
                deps = war if i == 0 else []
                rdeps = []
                for b in reads:
                    rdeps += b.w
                iid = P.op("pe",
                           (lambda e, lf=lf, rf=rf, st=(i == 0), sp=(i == n - 1), oa=oa:
                            e.matmul(oa, lf(), rf(), start=st, stop=sp)),
                           deps=deps + rdeps)
                for b in reads:
                    b.r.append(iid)
                last = iid
            bb.w = [last]
            bb.r = []

        for jj in range(min(nt, 2)):
            xl0 = xlnt[jj % 2]
            op("pool", lambda e, jj=jj, xl0=xl0: e.dma_start(
                xl0[:, :].rearrange("p (c t) -> p c t", c=NCH_D),
                xT[:, TT * jj:TT * (jj + 1)].rearrange("(c p) t -> p c t", p=128)),
                writes=B_xln[jj % 2], dma=True, sem=f"xln{jj % 2}")
        op("pool", lambda e: e.dma_start(bcd_bfd[:, :], bcd_w[:, :]),
           writes=[B_cast["misc"]], dma=True, sem="c_misc")
        op("pool", lambda e: e.dma_start(gate_bfd[:, :], gate_wl[:, :]),
           writes=[B_cast["misc"]], dma=True, sem="c_misc")



        for fn in (
            lambda e: e.dma_start(c_eye[:, :], eye[:, :]),
            lambda e: e.dma_start(c_in_b[:, :], in_bl[:, :]),
            lambda e: e.dma_start(c_dt_b[:, :], dt_bl[:, :]),
            lambda e: e.dma_start(c_out_b[:, :], out_bl[:, :]),
            lambda e: e.dma_start(c_conv_w[:, :], conv_wl[:, :]),
            lambda e: e.dma_start(c_conv_b[:, :], conv_bl[:, :]),
            lambda e: e.dma_start(c_bcd_b[:, :], bcd_b[:, :]),
            lambda e: e.dma_start(c_betm_row[:, :], bias_e_tm[:, :]),
        ):
            op("sp", fn, writes=[B_const], dma=True, sem="spc")
        op("sp", lambda e: e.dma_start(c_bcd_bf[:, :], bcd_bfd[:, :]),
           reads=[B_cast["misc"]], writes=[B_c_bcd], dma=True, sem="spc")
        op("sp", lambda e: e.dma_start(c_gate_bf[:, :], gate_bfd[:, :]),
           reads=[B_cast["misc"]], writes=[B_c_gate], dma=True, sem="spc")
        for v, ap in ((1.0, c_ones_bf), (1.0, c_ones1x16), (1.0, c_ones16),
                      (1.0, c_ones1x128), (1.0 / 1024.0, c_inv1024), (1.0 / 2048.0, c_inv2048),
                      (EPS, c_eps), (1.0, c_one)):
            op("dve", lambda e, v=v, ap=ap: e.memset(ap[:, :], v), writes=[B_const])
        op("dve", lambda e: e.memset(s0[:, :], 0.0), writes=[B_small["s0"]])
        mm_group(5, [((lambda: c_ones1x128[0:1, :]), (lambda: c_betm_row[0:1, :]), [B_const])],
                 out_sub=pb[5][:, 0:64])
        op("dve", lambda e: e.tensor_copy(c_bias_e_tm[:, :], pb[5][:, 0:64]),
           reads=[B_pb[5]], writes=[B_c_betm])
        for k in range(NCH_E):
            op("dve", lambda e, k=k: e.memset(xmt[0][:, XMC * k:XMC * k + 1], 0.0),
               writes=[B_xm[0][k]])

        ring_slot = [0]
        def next_rw():
            s = ring_slot[0] % 5
            ring_slot[0] += 1
            return s
        rdn_slot = [0]
        def next_rdn():
            s = rdn_slot[0] % 3
            rdn_slot[0] += 1
            return s

        def wload(j, ap_f32, ap_scr, scr_buf, three_d=False):
            s = next_rw()
            dst = (rw[:, 2048 * s:2048 * (s + 1)].rearrange("p (E x) -> p E x", E=2)
                   if three_d else rw[:, 2048 * s:2048 * (s + 1)])
            if j == 0:
                op("pool", lambda e, dst=dst, a=ap_f32: e.dma_start(dst, a),
                   writes=[B_rw[s]], dma=True, sem=f"rwc{s}")
                op("sp", lambda e, dst=dst, a=ap_scr: e.dma_start(a, dst),
                   reads=[B_rw[s]], writes=[scr_buf], dma=True, sem="wb")
            else:
                op("sp", lambda e, dst=dst, a=ap_scr: e.dma_start(dst, a),
                   reads=[scr_buf], writes=[B_rw[s]], dma=True, sem=f"rw{s}")
            return s

        def load_x(j):
            xl = xlnt[j % 2]
            op("pool", lambda e, j=j, xl=xl: e.dma_start(
                xl[:, :].rearrange("p (c t) -> p c t", c=NCH_D),
                xT[:, TT * j:TT * (j + 1)].rearrange("(c p) t -> p c t", p=128)),
                writes=B_xln[j % 2], dma=True, sem=f"xln{j % 2}")

        def ln_stats_inline(src_bufs, src_ap, nch, f32src, sb0=0, sb1=1):
            assert not f32src
            war0 = list(B_pb[sb0].w) + list(B_pb[sb0].r)
            war1 = list(B_pb[sb1].w) + list(B_pb[sb1].r)
            last0 = last1 = None

            def stat_mms(c):
                nonlocal last0, last1
                cs = c % 2
                d0 = war0 if c == 0 else []
                last0 = P.op("pe", (lambda e, c=c, st=(c == 0), sp=(c == nch - 1), sb0=sb0:
                                    e.matmul(pb[sb0][0:1, :], c_inv1024[:, :], src_ap(c),
                                             start=st, stop=sp)),
                             deps=d0 + src_bufs[c].w + B_const.w)
                src_bufs[c].r.append(last0)
                d1 = war1 if c == 0 else []
                last1 = P.op("pe", (lambda e, cs=cs, st=(c == 0), sp=(c == nch - 1), sb1=sb1:
                                    e.matmul(pb[sb1][0:1, :], c_inv1024[:, :],
                                             sq[:, TT * cs:TT * (cs + 1)], start=st, stop=sp)),
                             deps=d1 + B_sq[cs].w + B_const.w)
                B_sq[cs].r.append(last1)

            for c in range(nch):
                cs = c % 2
                if c >= 2:
                    stat_mms(c - 2)
                op("dve", lambda e, c=c, cs=cs: e.tensor_mul(
                    sq[:, TT * cs:TT * (cs + 1)], src_ap(c), src_ap(c)),
                    reads=[src_bufs[c]], writes=[B_sq[cs]])
            stat_mms(nch - 2)
            stat_mms(nch - 1)
            B_pb[sb0].w = [last0]
            B_pb[sb0].r = []
            B_pb[sb1].w = [last1]
            B_pb[sb1].r = []

        def ao_stats(j):
            ln_stats_inline(B_aob, lambda c: aob[:, TT * c:TT * (c + 1)], NCH_D, False)

        def ln_rows(sb0=0, sb1=1):
            op("act", lambda e, sb0=sb0: e.activation(rowA[:, :], pb[sb0][0:1, :], AF.Square),
               reads=[B_pb[sb0]], writes=[B_small["rowA"]])
            op("dve", lambda e, sb1=sb1: e.tensor_sub(rowB[:, :], pb[sb1][0:1, :], rowA[:, :]),
               reads=[B_pb[sb1], B_small["rowA"]], writes=[B_small["rowB"]])
            op("act", lambda e: e.activation(rowB[:, :], rowB[:, :], AF.Ln, bias=c_eps[0:1, :]),
               reads=[B_small["rowB"], B_const], writes=[B_small["rowB"]])
            op("act", lambda e: e.activation(rowB[:, :], rowB[:, :], AF.Exp, scale=-0.5),
               reads=[B_small["rowB"]], writes=[B_small["rowB"]])
            op("dve", lambda e, sb0=sb0: e.tensor_mul(rowA[:, :], rowB[:, :], pb[sb0][0:1, :]),
               reads=[B_small["rowB"], B_pb[sb0]], writes=[B_small["rowA"]])
            op("dve", lambda e: e.tensor_copy(rowBb[:, :], rowB[:, :]),
               reads=[B_small["rowB"]], writes=[B_small["rowBb"]])
            op("dve", lambda e: e.tensor_copy(rowAb[:, :], rowA[:, :]),
               reads=[B_small["rowA"]], writes=[B_small["rowAb"]])
            mm_group(4, [((lambda: c_ones_bf[0:1, :]), (lambda: rowBb[0:1, :]),
                          [B_small["rowBb"], B_const])])
            mm_group(5, [((lambda: c_ones_bf[0:1, :]), (lambda: rowAb[0:1, :]),
                          [B_small["rowAb"], B_const])])
            op("dve", lambda e: e.tensor_copy(rstd_sb[:, :], pb[4][:, :]),
               reads=[B_pb[4]], writes=[B_bc1])
            op("dve", lambda e: e.tensor_copy(mrstd_sb[:, :], pb[5][:, :]),
               reads=[B_pb[5]], writes=[B_bc2])

        def ln1_stats(j):
            xl = xlnt[j % 2]
            ln_stats_inline(B_xln[j % 2], lambda c: xl[:, TT * c:TT * (c + 1)], NCH_D, False)

        def ln1_main(j):
            xl = xlnt[j % 2]
            ln_rows()
            h1 = h1t[j % 2]
            bh1 = B_h1[j % 2]
            for w0 in range(0, NCH_D, 4):
                for c in range(w0, w0 + 4):
                    et = c % 4
                    eng_ = "dve" if c % 2 == 0 else "pool"
                    op(eng_, lambda e, c=c, et=et, xl=xl: e.tensor_mul(
                        etmp[:, TT * et:TT * (et + 1)], xl[:, TT * c:TT * (c + 1)], rstd_sb[:, :]),
                        reads=[B_xln[j % 2][c], B_bc1], writes=[B_etmp[et]])
                for c in range(w0, w0 + 4):
                    et = c % 4
                    eng_ = "dve" if c % 2 == 0 else "pool"
                    op(eng_, lambda e, c=c, et=et: e.tensor_sub(
                        h1[:, TT * c:TT * (c + 1)], etmp[:, TT * et:TT * (et + 1)], mrstd_sb[:, :]),
                        reads=[B_etmp[et], B_bc2], writes=[bh1[c]])
            xm = xmt[j % 2]
            bxm = B_xm[j % 2]
            for pc in range(8):
                slot = wload(j, in_wt[2 * pc:2 * pc + 2].rearrange("E p x -> p E x"),
                             in_bf[2 * pc:2 * pc + 2].rearrange("E p x -> p E x"),
                             B_scr["in"][pc], three_d=True)
                for h in range(2):
                    k = 2 * pc + h
                    bank = (2, 3, 6, 7)[k % 4]
                    parts = []
                    for c in range(NCH_D):
                        parts.append((
                            (lambda s=slot, h=h, c=c:
                             rw[:, 2048 * s + 1024 * h + 128 * c:2048 * s + 1024 * h + 128 * (c + 1)]),
                            (lambda c=c, h1=h1: h1[:, TT * c:TT * (c + 1)]),
                            [B_rw[slot], bh1[c]]))
                    mm_group(bank, parts)
                    op("dve", lambda e, k=k, bank=bank, xm=xm: e.tensor_scalar(
                        xm[:, XMC * k + 1:XMC * k + 513], pb[bank][:, :],
                        c_in_b[:, k:k + 1], None, ALU.add),
                        reads=[B_pb[bank], B_const], writes=[bxm[k]])

        def halos(j):
            a, b = xmt[j % 2], xmt[(j + 1) % 2]
            ba, bb = B_xm[j % 2], B_xm[(j + 1) % 2]
            for k in range(NCH_E):
                op("dve", lambda e, k=k, a=a, b=b: e.tensor_copy(
                    b[:, XMC * k:XMC * k + 1], a[:, XMC * k + 512:XMC * k + 513]),
                    reads=[ba[k]], writes=[bb[k]])
                op("dve", lambda e, k=k, a=a, b=b: e.tensor_copy(
                    a[:, XMC * k + 513:XMC * k + 514], b[:, XMC * k + 1:XMC * k + 2]),
                    reads=[bb[k]], writes=[ba[k]])

        def right_edge(j):
            a = xmt[j % 2]
            ba = B_xm[j % 2]
            for k in range(NCH_E):
                op("dve", lambda e, k=k, a=a: e.memset(a[:, XMC * k + 513:XMC * k + 514], 0.0),
                   writes=[ba[k]])

        def conv(j):
            a = xmt[j % 2]
            ba = B_xm[j % 2]
            for k in range(NCH_E):
                et = k % 2
                ea = etmp16[:, TT * et:TT * (et + 1)]
                op("dve", lambda e, k=k, a=a, ea=ea: e.tensor_scalar(
                    ea, a[:, XMC * k:XMC * k + 512], c_conv_w[:, 3 * k:3 * k + 1], None, ALU.mult),
                    reads=[ba[k], B_const], writes=[B_et16[et]])
                op("dve", lambda e, k=k, a=a, ea=ea: e.scalar_tensor_tensor(
                    ea, a[:, XMC * k + 1:XMC * k + 513], c_conv_w[:, 3 * k + 1:3 * k + 2], ea,
                    ALU.mult, ALU.add),
                    reads=[ba[k], B_const, B_et16[et]], writes=[B_et16[et]])
                op("dve", lambda e, k=k, a=a, ea=ea: e.scalar_tensor_tensor(
                    ea, a[:, XMC * k + 2:XMC * k + 514], c_conv_w[:, 3 * k + 2:3 * k + 3], ea,
                    ALU.mult, ALU.add),
                    reads=[ba[k], B_const, B_et16[et]], writes=[B_et16[et]])
                op("act", lambda e, k=k, a=a, ea=ea: e.activation(
                    a[:, XMC * k + 1:XMC * k + 513], ea, AF.Silu, bias=c_conv_b[:, k:k + 1]),
                    reads=[B_et16[et], B_const], writes=[ba[k]])

        def xc_ap(j, c):
            return xmt[j % 2][:, XMC * c + 1:XMC * c + 513]

        def dt_phase(j):
            bxm = B_xm[j % 2]
            war0 = list(B_pb[0].w) + list(B_pb[0].r)
            last0 = None

            def dt_sum_mm(O):
                nonlocal last0
                d0 = war0 if O == 0 else []
                last0 = P.op("pe", (lambda e, O=O, st=(O == 0), sp=(O == 15):
                                    e.matmul(pb[0][0:1, :], c_inv2048[:, :],
                                             dtb[:, TT * (O % 2):TT * (O % 2 + 1)],
                                             start=st, stop=sp)),
                             deps=d0 + B_dtb[O % 2].w + B_const.w)
                B_dtb[O % 2].r.append(last0)

            for O in range(16):
                slot = wload(j, dt_wt[O], dt_bf[O], B_scr["dt"][O])
                bank = (2, 3, 6, 7)[O % 4]
                parts = []
                for c in range(NCH_E):
                    parts.append((
                        (lambda s=slot, c=c: rw[:, 2048 * s + 128 * c:2048 * s + 128 * (c + 1)]),
                        (lambda c=c, j=j: xc_ap(j, c)),
                        [B_rw[slot], bxm[c]]))
                mm_group(bank, parts)
                if O >= 2:
                    dt_sum_mm(O - 2)
                et = 2 + (O % 2)
                ea = etmp[:, TT * et:TT * (et + 1)]
                op("act", lambda e, O=O, bank=bank, ea=ea: e.activation(
                    ea, pb[bank][:, :], AF.Exp, bias=c_dt_b[:, O:O + 1]),
                    reads=[B_pb[bank], B_const], writes=[B_etmp[et]])
                op("act", lambda e, O=O, ea=ea: e.activation(
                    dtb[:, TT * (O % 2):TT * (O % 2 + 1)], ea, AF.Ln, bias=c_one[:, :]),
                    reads=[B_etmp[et], B_const], writes=[B_dtb[O % 2]])
            dt_sum_mm(14)
            dt_sum_mm(15)
            B_pb[0].w = [last0]
            B_pb[0].r = []

        def bcd_scan(j):
            bxm = B_xm[j % 2]

            parts = []
            for c in range(NCH_E):
                parts.append((
                    (lambda c=c: c_bcd_bf[:, 80 * c:80 * (c + 1)]),
                    (lambda c=c, j=j: xc_ap(j, c)),
                    [B_c_bcd, bxm[c]]))
            mm_group(6, parts, out_sub=pb[6][0:80, :])
            op("dve", lambda e: e.tensor_scalar(
                u16[:, :], pb[6][0:16, :], c_bcd_b[:, 0:1], None, ALU.add),
                reads=[B_pb[6], B_const], writes=[B_small["u16"]])
            op("dve", lambda e: e.tensor_scalar(
                tmp16[:, :], pb[6][64:80, :], c_bcd_b[:, 2:3], None, ALU.add),
                reads=[B_pb[6], B_const], writes=[B_small["tmp16"]])
            op("dve", lambda e: e.tensor_mul(u16[:, :], u16[:, :], tmp16[:, :]),
               reads=[B_small["u16"], B_small["tmp16"]], writes=[B_small["u16"]])
            op("dve", lambda e: e.tensor_scalar(
                ct16[:, :], pb[6][32:48, :], c_bcd_b[:, 1:2], None, ALU.add),
                reads=[B_pb[6], B_const], writes=[B_small["ct16"]])
            op("dve", lambda e: e.tensor_scalar(rowA[:, :], pb[0][0:1, :], 10.0, -1.0,
                                                ALU.min, ALU.mult),
               reads=[B_pb[0]], writes=[B_small["rowA"]])
            if debug and j == 0:
                op("pool", lambda e: e.dma_start(dbg["ld"][:, :], rowA[:, :]),
                   reads=[B_small["rowA"]], dma=True, sem="dbgq")
            op("act", lambda e: e.activation(rowB[:, :], rowA[:, :], AF.Exp),
               reads=[B_small["rowA"]], writes=[B_small["rowB"]])
            mm_group(6, [((lambda: c_ones1x16[0:1, :]), (lambda: rowB[0:1, :]),
                          [B_small["rowB"], B_const])], out_sub=pb[6][0:16, :])
            op("dve", lambda e: e.tensor_tensor_scan(scano[:, :], pb[6][0:16, :], u16[:, :],
                                                     s0[:, :], ALU.mult, ALU.add),
               reads=[B_pb[6], B_small["u16"], B_small["s0"]], writes=[B_small["scano"]])
            op("dve", lambda e: e.tensor_copy(s0[:, :], scano[:, 511:512]),
               reads=[B_small["scano"]], writes=[B_small["s0"]])
            if debug and j == 0:
                op("pool", lambda e: e.dma_start(dbg["scan"][:, :], scano[:, :]),
                   reads=[B_small["scano"]], dma=True, sem="dbgq")
            op("dve", lambda e: e.tensor_mul(ct16[:, :], scano[:, :], ct16[:, :]),
               reads=[B_small["scano"], B_small["ct16"]], writes=[B_small["ct16"]])
            mm_group(0, [((lambda: c_ones16[:, :]), (lambda: ct16[:, :]),
                          [B_small["ct16"], B_const])], out_sub=pb[0][0:1, :])
            op("dve", lambda e: e.tensor_copy(rowAb[:, :], pb[0][0:1, :]),
               reads=[B_pb[0]], writes=[B_small["rowAb"]])
            mm_group(4, [((lambda: c_ones_bf[0:1, :]), (lambda: rowAb[0:1, :]),
                          [B_small["rowAb"], B_const])])

        def outproj_round(j, r):
            banks = (6, 7) if r % 2 == 0 else (2, 3)
            slots = [wload(j, out_wt[2 * r + q], out_bf[2 * r + q], B_scr["out"][2 * r + q])
                     for q in range(2)]
            wars = {q: list(B_pb[banks[q]].w) + list(B_pb[banks[q]].r) for q in range(2)}
            lasts = {}
            for k in range(NCH_E):
                for q in range(2):
                    reads = [B_rw[slots[q]], B_sg[k]]
                    deps = wars[q] if k == 0 else []
                    rdeps = []
                    for b in reads:
                        rdeps += b.w
                    iid = P.op("pe", (lambda e, sl=slots[q], k=k,
                                      st=(k == 0), sp=(k == NCH_E - 1), bank=banks[q]:
                                      e.matmul(pb[bank][:, :],
                                               rw[:, 2048 * sl + 128 * k:2048 * sl + 128 * (k + 1)],
                                               sg[:, TT * k:TT * (k + 1)],
                                               start=st, stop=sp)),
                               deps=deps + rdeps)
                    for b in reads:
                        b.r.append(iid)
                    lasts[q] = iid
            for q in range(2):
                bank = banks[q]
                B_pb[bank].w = [lasts[q]]
                B_pb[bank].r = []
                oc = 2 * r + q
                op("dve", lambda e, oc=oc, bank=bank: e.tensor_copy(
                    zs[:, TT * oc:TT * (oc + 1)], pb[bank][:, :]),
                    reads=[B_pb[bank]], writes=[B_zs[oc]])

        def ao_finalize(j):
            h1 = h1t[j % 2]
            bh1 = B_h1[j % 2]
            op("dve", lambda e: e.tensor_copy(ys_sb[:, :], pb[4][:, :]),
               reads=[B_pb[4]], writes=[B_ys])
            for w0 in range(0, NCH_D, 4):
                for oc in range(w0, w0 + 4):
                    et = oc % 4
                    op("dve", lambda e, oc=oc, et=et: e.tensor_mul(
                        etmp[:, TT * et:TT * (et + 1)], zs[:, TT * oc:TT * (oc + 1)], ys_sb[:, :]),
                        reads=[B_zs[oc], B_ys], writes=[B_etmp[et]])
                for oc in range(w0, w0 + 4):
                    et = oc % 4
                    op("dve", lambda e, oc=oc, et=et, h1=h1: e.scalar_tensor_tensor(
                        aob[:, TT * oc:TT * (oc + 1)], etmp[:, TT * et:TT * (et + 1)],
                        c_out_b[:, oc:oc + 1], h1[:, TT * oc:TT * (oc + 1)], ALU.add, ALU.add),
                        reads=[B_etmp[et], B_const, bh1[oc]], writes=[B_aob[oc]])
            if debug and j == 0:
                for oc in range(NCH_D):
                    op("pool", lambda e, oc=oc: e.dma_start(
                        dbg["ao"][:, TT * oc:TT * (oc + 1)], aob[:, TT * oc:TT * (oc + 1)]),
                        reads=[B_aob[oc]], dma=True, sem="dbgq")

        def ln3_residual(j):
            ln_rows()
            xl = xlnt[j % 2]
            for w0 in range(0, NCH_D, 4):
                for oc in range(w0, w0 + 4):
                    et = oc % 4
                    eng_ = "dve" if oc % 2 == 0 else "pool"
                    op(eng_, lambda e, oc=oc, et=et: e.tensor_mul(
                        etmp[:, TT * et:TT * (et + 1)], aob[:, TT * oc:TT * (oc + 1)], rstd_sb[:, :]),
                        reads=[B_aob[oc], B_bc1], writes=[B_etmp[et]])
                for oc in range(w0, w0 + 4):
                    et = oc % 4
                    eng_ = "dve" if oc % 2 == 0 else "pool"
                    op(eng_, lambda e, et=et: e.tensor_sub(
                        etmp[:, TT * et:TT * (et + 1)], etmp[:, TT * et:TT * (et + 1)], mrstd_sb[:, :]),
                        reads=[B_etmp[et], B_bc2], writes=[B_etmp[et]])
                for oc in range(w0, w0 + 4):
                    et = oc % 4
                    eng_ = "pool" if oc % 2 == 0 else "dve"
                    op(eng_, lambda e, oc=oc, et=et, xl=xl: e.tensor_add(
                        x2[:, TT * oc:TT * (oc + 1)], etmp[:, TT * et:TT * (et + 1)],
                        xl[:, TT * oc:TT * (oc + 1)]),
                        reads=[B_etmp[et], B_xln[j % 2][oc]], writes=[B_x2[oc]])
            if debug and j == 0:
                for oc in range(NCH_D):
                    op("pool", lambda e, oc=oc: e.dma_start(
                        dbg["x2"][:, TT * oc:TT * (oc + 1)], x2[:, TT * oc:TT * (oc + 1)]),
                        reads=[B_x2[oc]], dma=True, sem="dbgq")

        def ln4(j):
            ln_stats_inline(B_x2, lambda c: x2[:, TT * c:TT * (c + 1)], NCH_D, False)
            ln_rows()
            for w0 in range(0, NCH_D, 4):
                for c in range(w0, w0 + 4):
                    et = c % 4
                    eng_ = "dve" if c % 2 == 0 else "pool"
                    op(eng_, lambda e, c=c, et=et: e.tensor_mul(
                        etmp[:, TT * et:TT * (et + 1)], x2[:, TT * c:TT * (c + 1)], rstd_sb[:, :]),
                        reads=[B_x2[c], B_bc1], writes=[B_etmp[et]])
                for c in range(w0, w0 + 4):
                    et = c % 4
                    eng_ = "dve" if c % 2 == 0 else "pool"
                    op(eng_, lambda e, c=c, et=et: e.tensor_sub(
                        h2[:, TT * c:TT * (c + 1)], etmp[:, TT * et:TT * (et + 1)], mrstd_sb[:, :]),
                        reads=[B_etmp[et], B_bc2], writes=[B_h2[c]])
            if debug and j == 0:
                for c in range(NCH_D):
                    op("pool", lambda e, c=c: e.dma_start(
                        dbg["h2"][:, TT * c:TT * (c + 1)], h2[:, TT * c:TT * (c + 1)]),
                        reads=[B_h2[c]], dma=True, sem="dbgq")

        def inprojB(j, half=None):
            h1 = h1t[j % 2]
            bh1 = B_h1[j % 2]
            rng = range(8) if half is None else (range(4) if half == 0 else range(4, 8))
            for pc in rng:
                slot = wload(j, in_wt[16 + 2 * pc:16 + 2 * pc + 2].rearrange("E p x -> p E x"),
                             in_bf[16 + 2 * pc:16 + 2 * pc + 2].rearrange("E p x -> p E x"),
                             B_scr["in"][8 + pc], three_d=True)
                for h in range(2):
                    k = 2 * pc + h
                    bank = (2, 3, 6, 7)[k % 4]
                    parts = []
                    for c in range(NCH_D):
                        parts.append((
                            (lambda s=slot, h=h, c=c:
                             rw[:, 2048 * s + 1024 * h + 128 * c:2048 * s + 1024 * h + 128 * (c + 1)]),
                            (lambda c=c, h1=h1: h1[:, TT * c:TT * (c + 1)]),
                            [B_rw[slot], bh1[c]]))
                    mm_group(bank, parts)
                    op("act", lambda e, k=k, bank=bank: e.activation(
                        sg[:, TT * k:TT * (k + 1)], pb[bank][:, :], AF.Silu,
                        bias=c_in_b[:, 16 + k:17 + k]),
                        reads=[B_pb[bank], B_const], writes=[B_sg[k]])

        def router_logits(j):
            parts = []
            for c in range(NCH_D):
                parts.append((
                    (lambda c=c: c_gate_bf[:, 16 * c:16 * (c + 1)]),
                    (lambda c=c: h2[:, TT * c:TT * (c + 1)]),
                    [B_c_gate, B_h2[c]]))
            mm_group(6, parts, out_sub=pb[6][0:16, :])
            op("dve", lambda e: e.tensor_copy(lg16[:, :], pb[6][0:16, :]),
               reads=[B_pb[6]], writes=[B_small["lg16"]])
            tr_parts = []
            war5 = list(B_pb[5].w) + list(B_pb[5].r)
            last = None
            for c in range(4):
                deps = war5 if c == 0 else []
                last = P.op("pe", (lambda e, c=c: e.transpose(
                    pb[5][:, 16 * c:16 * (c + 1)], lg16[:, 128 * c:128 * (c + 1)],
                    c_eye[0:16, 0:16])),
                    deps=deps + B_small["lg16"].w + B_const.w)
                B_small["lg16"].r.append(last)
            B_pb[5].w = [last]
            B_pb[5].r = []

        def router_rest(j):
            op("act", lambda e: e.activation(scor_tm[:, :], pb[5][:, 0:64], AF.Exp, scale=-1.0),
               reads=[B_pb[5]], writes=[B_small["scor_tm"]])
            op("dve", lambda e: e.tensor_scalar(scor_tm[:, :], scor_tm[:, :], 1.0, None, ALU.add),
               reads=[B_small["scor_tm"]], writes=[B_small["scor_tm"]])
            op("dve", lambda e: e.reciprocal(scor_tm[:, :], scor_tm[:, :]),
               reads=[B_small["scor_tm"]], writes=[B_small["scor_tm"]])
            op("dve", lambda e: e.tensor_add(sc_tm[:, :], scor_tm[:, :], c_bias_e_tm[:, :]),
               reads=[B_small["scor_tm"], B_c_betm], writes=[B_small["sc_tm"]])
            if debug and j == 0:
                op("pool", lambda e: e.dma_start(dbg["scores"][:, :], scor_tm[:, :]),
                   reads=[B_small["scor_tm"]], dma=True, sem="dbgq")
            pairs = [(0, 1), (0, 2), (0, 3), (1, 2), (1, 3), (2, 3)]
            for n, (i, jj) in enumerate(pairs):
                if n == 0:
                    op("dve", lambda e, i=i, jj=jj: e.tensor_add(
                        gs[:, :], sc_tm[:, i::4], sc_tm[:, jj::4]),
                        reads=[B_small["sc_tm"]], writes=[B_small["gs"]])
                else:
                    op("dve", lambda e, i=i, jj=jj: e.tensor_add(
                        gpair[:, :], sc_tm[:, i::4], sc_tm[:, jj::4]),
                        reads=[B_small["sc_tm"]], writes=[B_small["gpair"]])
                    op("dve", lambda e: e.tensor_max(gs[:, :], gs[:, :], gpair[:, :]),
                       reads=[B_small["gs"], B_small["gpair"]], writes=[B_small["gs"]])
            for n, (i, jj) in enumerate(pairs):
                if n == 0:
                    op("dve", lambda e, i=i, jj=jj: e.tensor_tensor(
                        thr2[:, :], gs[:, i::4], gs[:, jj::4], ALU.min),
                        reads=[B_small["gs"]], writes=[B_small["thr2"]])
                else:
                    op("dve", lambda e, i=i, jj=jj: e.tensor_tensor(
                        mxc[:, :], gs[:, i::4], gs[:, jj::4], ALU.min),
                        reads=[B_small["gs"]], writes=[B_small["mxc"]])
                    op("dve", lambda e: e.tensor_max(thr2[:, :], thr2[:, :], mxc[:, :]),
                       reads=[B_small["thr2"], B_small["mxc"]], writes=[B_small["thr2"]])
            for c in range(4):
                op("dve", lambda e, c=c: e.tensor_scalar(
                    gmask[:, 4 * c:4 * (c + 1)], gs[:, 4 * c:4 * (c + 1)],
                    thr2[:, c:c + 1], None, ALU.is_ge),
                    reads=[B_small["gs"], B_small["thr2"]], writes=[B_small["gmask"]])
            for c in range(4):
                for g in range(4):
                    op("dve", lambda e, c=c, g=g: e.tensor_scalar(
                        maskd[:, 16 * c + 4 * g:16 * c + 4 * (g + 1)],
                        sc_tm[:, 16 * c + 4 * g:16 * c + 4 * (g + 1)],
                        gmask[:, 4 * c + g:4 * c + g + 1], None, ALU.mult),
                        reads=[B_small["sc_tm"], B_small["gmask"]], writes=[B_small["maskd"]])
            op("dve", lambda e: e.tensor_copy(maskw[:, :], maskd[:, :]),
               reads=[B_small["maskd"]], writes=[B_small["maskw"]])
            for r in range(3):
                for c in range(4):
                    op("dve", lambda e, c=c: e.reduce_max(
                        mxc[:, c:c + 1], maskw[:, 16 * c:16 * (c + 1)], axis=AX.X),
                        reads=[B_small["maskw"]], writes=[B_small["mxc"]])
                for c in range(4):
                    op("dve", lambda e, c=c: e.tensor_scalar(
                        cmpb[:, :], maskw[:, 16 * c:16 * (c + 1)], mxc[:, c:c + 1],
                        None, ALU.is_ge),
                        reads=[B_small["maskw"], B_small["mxc"]], writes=[B_small["cmpb"]])
                    op("dve", lambda e, c=c: e.scalar_tensor_tensor(
                        maskw[:, 16 * c:16 * (c + 1)], cmpb[:, :], -BIG,
                        maskw[:, 16 * c:16 * (c + 1)], ALU.mult, ALU.add),
                        reads=[B_small["cmpb"], B_small["maskw"]], writes=[B_small["maskw"]])
            for c in range(4):
                op("dve", lambda e, c=c: e.reduce_max(
                    mxc[:, c:c + 1], maskw[:, 16 * c:16 * (c + 1)], axis=AX.X),
                    reads=[B_small["maskw"]], writes=[B_small["mxc"]])
            for c in range(4):
                op("dve", lambda e, c=c: e.tensor_scalar(
                    selb[:, 16 * c:16 * (c + 1)], maskd[:, 16 * c:16 * (c + 1)],
                    mxc[:, c:c + 1], None, ALU.is_ge),
                    reads=[B_small["maskd"], B_small["mxc"]], writes=[B_small["selb"]])
            op("dve", lambda e: e.tensor_mul(wtm[:, :], scor_tm[:, :], selb[:, :]),
               reads=[B_small["scor_tm"], B_small["selb"]], writes=[B_small["wtm"]])
            for c in range(4):
                op("dve", lambda e, c=c: e.tensor_reduce(
                    den[:, c:c + 1], wtm[:, 16 * c:16 * (c + 1)], AX.X, ALU.add),
                    reads=[B_small["wtm"]], writes=[B_small["den"]])
            op("dve", lambda e: e.tensor_scalar(den[:, :], den[:, :], 1.0 / RSCALE,
                                                1e-20 / RSCALE, ALU.mult, ALU.add),
               reads=[B_small["den"]], writes=[B_small["den"]])
            op("dve", lambda e: e.reciprocal(den[:, :], den[:, :]),
               reads=[B_small["den"]], writes=[B_small["den"]])
            for c in range(4):
                op("dve", lambda e, c=c: e.tensor_scalar(
                    com_tm[:, 16 * c:16 * (c + 1)], wtm[:, 16 * c:16 * (c + 1)],
                    den[:, c:c + 1], None, ALU.mult),
                    reads=[B_small["wtm"], B_small["den"]], writes=[B_small["com_tm"]])
        def router_combine(j):
            war5 = list(B_pb[5].w) + list(B_pb[5].r)
            last = None
            for c in range(4):
                deps = war5 if c == 0 else []
                last = P.op("pe", (lambda e, c=c: e.transpose(
                    pb[5][0:16, 128 * c:128 * (c + 1)], com_tm[:, 16 * c:16 * (c + 1)],
                    c_eye[:, :])),
                    deps=deps + B_small["com_tm"].w + B_const.w)
                B_small["com_tm"].r.append(last)
            B_pb[5].w = [last]
            B_pb[5].r = []
            op("dve", lambda e: e.tensor_copy(com_em[:, :], pb[5][0:16, :]),
               reads=[B_pb[5]], writes=[B_small["com_em"]])
            if debug and j == 0:
                op("pool", lambda e: e.dma_start(dbg["combine"][:, :], com_em[:, :]),
                   reads=[B_small["com_em"]], dma=True, sem="dbgq")

        def cw_fetch_one(ridx):
            cs = ridx % 5
            op("act", lambda e, r=ridx, cs=cs: e.dma_start(
                cwrow[:, TT * cs:TT * (cs + 1)], com_em[r:r + 1, :]),
                reads=[B_small["com_em"]], writes=[B_cwrow[cs]], dma=True, sem=f"cw{cs}")

        def ffn_ups(j, g):
            if True:
                for eL in range(4):
                    es = 16 + eL if g == 0 else 4 * (g - 1) + eL
                    routed = g > 0
                    if routed:
                        ridx = 4 * (g - 1) + eL
                        if ridx + 2 < 16:
                            cw_fetch_one(ridx + 2)
                        cs = ridx % 5
                        cb = 4 + (eL % 2)
                        mm_group(cb, [((lambda: c_ones_bf[0:1, :]),
                                       (lambda cs=cs: cwrow[0:1, TT * cs:TT * (cs + 1)]),
                                       [B_cwrow[cs], B_const])])
                        op("dve", lambda e, cb=cb, cs2=eL % 2: e.tensor_copy(
                            cw_sb[:, TT * cs2:TT * (cs2 + 1)], pb[cb][:, :]),
                            reads=[B_pb[cb]], writes=[B_cw[eL % 2]])
                    for hh in range(2):
                        slot = wload(j, up_wt[es, hh], up_bf[es, hh], B_scr["up"][es][hh])
                        for ii in range(2):
                            ic = 2 * hh + ii
                            bank = (2, 3, 6, 7)[(4 * eL + ic) % 4]
                            parts = []
                            for c in range(NCH_D):
                                parts.append((
                                    (lambda s=slot, ii=ii, c=c:
                                     rw[:, 2048 * s + 1024 * ii + 128 * c:2048 * s + 1024 * ii + 128 * (c + 1)]),
                                    (lambda c=c: h2[:, TT * c:TT * (c + 1)]),
                                    [B_rw[slot], B_h2[c]]))
                            mm_group(bank, parts)
                            zi = 4 * eL + ic
                            et = ic % 2
                            ea = etmp16[:, TT * et:TT * (et + 1)]
                            op("act", lambda e, bank=bank, ea=ea: e.activation(
                                ea, pb[bank][:, :], AF.Relu),
                                reads=[B_pb[bank]], writes=[B_et16[et]])
                            if routed:
                                op("act", lambda e, ea=ea: e.activation(ea, ea, AF.Square),
                                   reads=[B_et16[et]], writes=[B_et16[et]])
                                op("dve", lambda e, zi=zi, ea=ea, cs2=eL % 2: e.tensor_mul(
                                    zs[:, TT * zi:TT * (zi + 1)], ea,
                                    cw_sb[:, TT * cs2:TT * (cs2 + 1)]),
                                    reads=[B_et16[et], B_cw[eL % 2]], writes=[B_zs[zi]])
                            else:
                                op("act", lambda e, zi=zi, ea=ea: e.activation(
                                    zs[:, TT * zi:TT * (zi + 1)], ea, AF.Square),
                                    reads=[B_et16[et]], writes=[B_zs[zi]])
        def ffn_downs(j, g):
            if True:
                for oc in range(8):
                    slot = next_rdn()
                    if j == 0:
                        op("pool", lambda e, s=slot, oc=oc, g=g: e.dma_start(
                            rdn[:, 2048 * s:2048 * (s + 1)], dn_wt[oc, g]),
                            writes=[B_rdn[slot]], dma=True, sem=f"rdnc{slot}")
                        op("sp", lambda e, s=slot, oc=oc, g=g: e.dma_start(
                            dn_bf[oc, g], rdn[:, 2048 * s:2048 * (s + 1)]),
                            reads=[B_rdn[slot]], writes=[B_scr["dn"][oc][g]], dma=True, sem="wb")
                    else:
                        op("sp", lambda e, s=slot, oc=oc, g=g: e.dma_start(
                            rdn[:, 2048 * s:2048 * (s + 1)], dn_bf[oc, g]),
                            reads=[B_scr["dn"][oc][g]], writes=[B_rdn[slot]], dma=True, sem=f"rdn{slot}")
                    bank = 6 + (oc % 2)
                    parts = []
                    for zi in range(16):
                        parts.append((
                            (lambda s=slot, zi=zi: rdn[:, 2048 * s + 128 * zi:2048 * s + 128 * (zi + 1)]),
                            (lambda zi=zi: zs[:, TT * zi:TT * (zi + 1)]),
                            [B_rdn[slot], B_zs[zi]]))
                    mm_group(bank, parts)
                    if g == 0:
                        op("dve", lambda e, oc=oc, bank=bank: e.tensor_copy(
                            aob[:, TT * oc:TT * (oc + 1)], pb[bank][:, :]),
                            reads=[B_pb[bank]], writes=[B_aob[oc]])
                    else:
                        op("dve", lambda e, oc=oc, bank=bank: e.tensor_add(
                            aob[:, TT * oc:TT * (oc + 1)], pb[bank][:, :],
                            aob[:, TT * oc:TT * (oc + 1)]),
                            reads=[B_pb[bank], B_aob[oc]], writes=[B_aob[oc]])
            if debug and j == 0 and g == NGRP - 1:
                for oc in range(NCH_D):
                    op("pool", lambda e, oc=oc: e.dma_start(
                        dbg["ffn"][:, TT * oc:TT * (oc + 1)], aob[:, TT * oc:TT * (oc + 1)]),
                        reads=[B_aob[oc]], dma=True, sem="dbgq")

        store_deps = []

        def ln5_stats(j):
            ln_stats_inline(B_aob, lambda c: aob[:, TT * c:TT * (c + 1)], NCH_D, False,
                            sb0=2, sb1=3)

        def ln5_apply_store(j):
            for w0 in range(0, NCH_D, 4):
                for oc in range(w0, w0 + 4):
                    et = oc % 4
                    eng_ = "dve" if oc % 2 == 0 else "pool"
                    op(eng_, lambda e, oc=oc, et=et: e.tensor_mul(
                        etmp[:, TT * et:TT * (et + 1)], aob[:, TT * oc:TT * (oc + 1)], rstd_sb[:, :]),
                        reads=[B_aob[oc], B_bc1], writes=[B_etmp[et]])
                for oc in range(w0, w0 + 4):
                    et = oc % 4
                    eng_ = "dve" if oc % 2 == 0 else "pool"
                    op(eng_, lambda e, et=et: e.tensor_sub(
                        etmp[:, TT * et:TT * (et + 1)], etmp[:, TT * et:TT * (et + 1)], mrstd_sb[:, :]),
                        reads=[B_etmp[et], B_bc2], writes=[B_etmp[et]])
                for oc in range(w0, w0 + 4):
                    et = oc % 4
                    os_ = oc % 2
                    eng_ = "pool" if oc % 2 == 0 else "dve"
                    op(eng_, lambda e, oc=oc, et=et, os_=os_: e.tensor_add(
                        ostage[:, TT * os_:TT * (os_ + 1)], etmp[:, TT * et:TT * (et + 1)],
                        x2[:, TT * oc:TT * (oc + 1)]),
                        reads=[B_etmp[et], B_x2[oc]], writes=[B_ostage[os_]])
                    sid = op("pool", lambda e, oc=oc, j=j, os_=os_: e.dma_start(
                        outT[128 * oc:128 * (oc + 1), TT * j:TT * (j + 1)],
                        ostage[:, TT * os_:TT * (os_ + 1)]),
                        reads=[B_ostage[os_]], dma=True, sem="st")
                    store_deps.append(sid)

        ln1_stats(0)
        ln1_main(0)
        inprojB(0)
        if nt > 1:
            ln1_stats(1)
        if debug:
            for c in range(NCH_D):
                op("pool", lambda e, c=c: e.dma_start(
                    dbg["h1"][:, TT * c:TT * (c + 1)], h1t[0][:, TT * c:TT * (c + 1)]),
                    reads=[B_h1[0][c]], dma=True, sem="dbgq")
        for j in range(nt):
            if j > 0:
                ln_rows(sb0=2, sb1=3)
                ln5_apply_store(j - 1)
            outproj_round(j, 0)
            outproj_round(j, 1)
            if j + 1 < nt:
                ln1_main(j + 1)
                halos(j)
            else:
                right_edge(j)
            conv(j)
            if debug and j == 0:
                for k in range(NCH_E):
                    op("pool", lambda e, k=k: e.dma_start(
                        dbg["xc"][:, XMC * k:XMC * k + 514], xmt[0][:, XMC * k:XMC * k + 514]),
                        reads=[B_xm[0][k]], dma=True, sem="dbgq")
            outproj_round(j, 2)
            dt_phase(j)
            bcd_scan(j)
            outproj_round(j, 3)
            if j + 1 < nt:
                inprojB(j + 1, half=0)
            ao_finalize(j)
            ao_stats(j)
            if j + 1 < nt:
                inprojB(j + 1, half=1)
            ln3_residual(j)
            if j + 2 < nt:
                load_x(j + 2)
            ln4(j)
            router_logits(j)
            ffn_ups(j, 0)
            router_rest(j)
            ffn_downs(j, 0)
            router_combine(j)
            cw_fetch_one(0)
            cw_fetch_one(1)
            for g in range(1, NGRP):
                ffn_ups(j, g)
                ffn_downs(j, g)
            ln5_stats(j)
            if j + 2 < nt:
                ln1_stats(j + 2)

        ln_rows(sb0=2, sb1=3)
        ln5_apply_store(nt - 1)

        all_dma_deps = [i for i, o in enumerate(P.ops) if o["dma"]]
        op("sp", lambda e: e.nop(), deps=store_deps + all_dma_deps)
        counts = P.finalize_and_emit(nc, make_sem)
    return nc, counts, dbg


def host_layout(in_w, in_b, conv_w, conv_b, dt_w, dt_b, B_w, B_b, C_w, C_b,
                D_w, D_b, out_w, out_b, gate_w, bias_e, up_w, down_w,
                shared_up, shared_down):
    f = np.ascontiguousarray
    d = {}
    d["in_wt"] = f(in_w.reshape(32, 128, 8, 128).transpose(0, 3, 2, 1).reshape(32, 128, 1024))
    d["dt_wt"] = f(dt_w.reshape(16, 128, 16, 128).transpose(0, 3, 2, 1).reshape(16, 128, 2048))
    d["out_wt"] = f(out_w.reshape(8, 128, 16, 128).transpose(0, 3, 2, 1).reshape(8, 128, 2048))
    upr = up_w.reshape(16, 4, 128, 8, 128).transpose(0, 1, 4, 3, 2).reshape(16, 4, 128, 1024)
    ups = shared_up.reshape(4, 4, 128, 8, 128).transpose(0, 1, 4, 3, 2).reshape(4, 4, 128, 1024)
    up_all = np.concatenate([upr, ups], axis=0)
    d["up_wt"] = f(up_all.reshape(20, 2, 2, 128, 1024).transpose(0, 1, 3, 2, 4)
                   .reshape(20, 2, 128, 2048))
    dnr = down_w.reshape(16, 8, 128, 4, 128).transpose(1, 0, 4, 3, 2).reshape(8, 16, 128, 512)
    dns = shared_down.reshape(8, 128, 4, 4, 128).transpose(0, 2, 4, 3, 1).reshape(8, 4, 128, 512)
    dn_all = np.concatenate([dns, dnr], axis=1)
    d["dn_wt"] = f(dn_all.reshape(8, 5, 4, 128, 512).transpose(0, 1, 3, 2, 4)
                   .reshape(8, 5, 128, 2048))
    bcd = np.zeros((128, 16, 80), np.float32)
    for i, w in enumerate((B_w, C_w, D_w)):
        bcd[:, :, 32 * i:32 * i + 16] = w.T.reshape(16, 128, 16).transpose(1, 0, 2)
    d["bcd_w"] = f(bcd.reshape(128, 1280))
    d["gate_wl"] = f(gate_w.T.reshape(8, 128, 16).transpose(1, 0, 2).reshape(128, 128))
    d["conv_wl"] = f(conv_w.reshape(16, 128, 3).transpose(1, 0, 2).reshape(128, 48))
    d["conv_bl"] = f(conv_b.reshape(16, 128).T)
    d["in_bl"] = f(in_b.reshape(32, 128).T)
    d["dt_bl"] = f(dt_b.reshape(16, 128).T)
    d["out_bl"] = f(out_b.reshape(8, 128).T)
    d["bcd_b"] = f(np.stack([B_b, C_b, D_b], axis=1))
    d["bias_e_tm"] = f(np.tile(bias_e[None, :], (1, 4)))
    d["eye"] = np.eye(128, dtype=np.float32)
    return d


_CACHED = {}


def kernel(L0_attn_x, L0_attn_in_w, L0_attn_in_b, L0_attn_conv_w, L0_attn_conv_b,
           L0_attn_dt_w, L0_attn_dt_b, L0_attn_B_w, L0_attn_B_b, L0_attn_C_w,
           L0_attn_C_b, L0_attn_D_w, L0_attn_D_b, L0_attn_out_w, L0_attn_out_b,
           L0_ffn_gate_w, L0_ffn_bias_e, L0_ffn_up_w, L0_ffn_down_w,
           L0_ffn_shared_up, L0_ffn_shared_down):
    x = np.asarray(L0_attn_x, dtype=np.float32)
    w = host_layout(np.asarray(L0_attn_in_w, np.float32), np.asarray(L0_attn_in_b, np.float32),
                    np.asarray(L0_attn_conv_w, np.float32), np.asarray(L0_attn_conv_b, np.float32),
                    np.asarray(L0_attn_dt_w, np.float32), np.asarray(L0_attn_dt_b, np.float32),
                    np.asarray(L0_attn_B_w, np.float32), np.asarray(L0_attn_B_b, np.float32),
                    np.asarray(L0_attn_C_w, np.float32), np.asarray(L0_attn_C_b, np.float32),
                    np.asarray(L0_attn_D_w, np.float32), np.asarray(L0_attn_D_b, np.float32),
                    np.asarray(L0_attn_out_w, np.float32), np.asarray(L0_attn_out_b, np.float32),
                    np.asarray(L0_ffn_gate_w, np.float32), np.asarray(L0_ffn_bias_e, np.float32),
                    np.asarray(L0_ffn_up_w, np.float32), np.asarray(L0_ffn_down_w, np.float32),
                    np.asarray(L0_ffn_shared_up, np.float32),
                    np.asarray(L0_ffn_shared_down, np.float32))
    if "nc" not in _CACHED:
        nc, counts, _ = build_program(nt=8, debug=False)
        _CACHED["nc"] = nc
    nc = _CACHED["nc"]
    in_maps = []
    for b in range(8):
        m = dict(w)
        m["xT"] = np.ascontiguousarray(x[b].T)
        in_maps.append(m)
    res = run_bass_kernel_spmd(nc, in_maps, core_ids=list(range(8)))
    out = np.stack([np.ascontiguousarray(r["outT"].T) for r in res.results], axis=0)
    return out


# version 27
# speedup vs baseline: 1.0270x; 1.0270x over previous
import numpy as np
from contextlib import ExitStack

import concourse.bass as bass
import concourse.mybir as mybir
from concourse.bass_utils import run_bass_kernel_spmd

F32 = mybir.dt.float32
BF16 = mybir.dt.bfloat16
FP16 = mybir.dt.float16
AF = mybir.ActivationFunctionType
ALU = mybir.AluOpType
AX = mybir.AxisListType

D = 1024
E = 2048
S = 16
TT = 512
NCH_D = 8
NCH_E = 16
XMC = 516
NEXP = 20
NGRP = 5
EPS = 1e-5
RSCALE = 2.5
BIG = 1.0e30

ENGINES = ("pe", "act", "dve", "pool", "sp")
BLOCK_ATTR = {"pe": "tensor", "act": "scalar", "dve": "vector", "pool": "gpsimd", "sp": "sync"}


class Buf:
    __slots__ = ("w", "r")

    def __init__(self):
        self.w = []
        self.r = []


class Plan:

    COARSE = ("spc", "c_misc", "wb", "xln0", "xln1", "st", "dbgq")

    def __init__(self):
        self.ops = []

    def op(self, eng, fn, reads=(), writes=(), deps=(), dma=False, sem=None):
        dl = [d for d in deps if d is not None]
        for b in reads:
            dl += b.w
        for b in writes:
            dl += b.w + b.r
        i = len(self.ops)
        if eng == "pe":
            dl = [d for d in dl if self.ops[d]["eng"] != "pe"]
        dl = sorted(set(dl))
        if dma:
            assert sem is not None
        self.ops.append({"eng": eng, "fn": fn, "deps": dl, "dma": dma,
                         "sem": sem if dma else eng, "inc": False, "val": None})
        for b in reads:
            b.r.append(i)
        for b in writes:
            b.w = [i]
            b.r = []
        return i

    def finalize_and_emit(self, nc, make_sem):
        for o in self.ops:
            if o["dma"]:
                o["inc"] = True
            for d in o["deps"]:
                self.ops[d]["inc"] = True
        semkeys = sorted({o["sem"] for o in self.ops})
        cnt = {k: 0 for k in semkeys}
        for o in self.ops:
            if o["inc"]:
                cnt[o["sem"]] += 16 if o["dma"] else 1
                o["val"] = cnt[o["sem"]]
        run = {k: 0 for k in semkeys}
        last = {e: {k: 0 for k in semkeys} for e in ENGINES}
        for o in self.ops:
            need = {}
            for d in o["deps"]:
                do = self.ops[d]
                k = do["sem"]
                v = run[k] if k in self.COARSE else do["val"]
                if v is None:
                    continue
                if v > need.get(k, 0):
                    need[k] = v
            me = o["eng"]
            w = []
            for k, v in need.items():
                if v > last[me][k]:
                    w.append((k, v))
                    last[me][k] = v
            o["waits"] = w
            if o["inc"]:
                run[o["sem"]] += 16 if o["dma"] else 1

        sems = {k: make_sem(k) for k in semkeys}
        per = {e: [o for o in self.ops if o["eng"] == e] for e in ENGINES}

        def runner(eng_name):
            def body(eng):
                for o in per[eng_name]:
                    for (k, v) in o["waits"]:
                        eng.wait_ge(sems[k], v)
                    instr = o["fn"](eng)
                    if o["inc"]:
                        instr.then_inc(sems[o["sem"]], 16 if o["dma"] else 1)
            return body

        with nc.Block() as block:
            for e in ENGINES:
                getattr(block, BLOCK_ATTR[e])(runner(e))
        return {e: len(per[e]) for e in ENGINES}


def build_program(nt=8, debug=False):
    nc = bass.Bass("TRN2", target_bir_lowering=False, debug=False)

    xT = nc.dram_tensor("xT", [D, 4096], F32, kind="ExternalInput").ap()
    in_wt = nc.dram_tensor("in_wt", [32, 128, 1024], F32, kind="ExternalInput").ap()
    dt_wt = nc.dram_tensor("dt_wt", [16, 128, 2048], F32, kind="ExternalInput").ap()
    out_wt = nc.dram_tensor("out_wt", [8, 128, 2048], F32, kind="ExternalInput").ap()
    up_wt = nc.dram_tensor("up_wt", [NEXP, 2, 128, 2048], F32, kind="ExternalInput").ap()
    dn_wt = nc.dram_tensor("dn_wt", [8, NGRP, 128, 2048], F32, kind="ExternalInput").ap()
    bcd_w = nc.dram_tensor("bcd_w", [128, 1280], F32, kind="ExternalInput").ap()
    gate_wl = nc.dram_tensor("gate_wl", [128, 128], F32, kind="ExternalInput").ap()
    conv_wl = nc.dram_tensor("conv_wl", [128, 48], F32, kind="ExternalInput").ap()
    conv_bl = nc.dram_tensor("conv_bl", [128, 16], F32, kind="ExternalInput").ap()
    in_bl = nc.dram_tensor("in_bl", [128, 32], F32, kind="ExternalInput").ap()
    dt_bl = nc.dram_tensor("dt_bl", [128, 16], F32, kind="ExternalInput").ap()
    out_bl = nc.dram_tensor("out_bl", [128, 8], F32, kind="ExternalInput").ap()
    bcd_b = nc.dram_tensor("bcd_b", [16, 3], F32, kind="ExternalInput").ap()
    bias_e_tm = nc.dram_tensor("bias_e_tm", [1, 64], F32, kind="ExternalInput").ap()
    eye = nc.dram_tensor("eye", [128, 128], F32, kind="ExternalInput").ap()

    outT = nc.dram_tensor("outT", [D, 4096], F32, kind="ExternalOutput").ap()

    in_bf = nc.dram_tensor("in_bf", [32, 128, 1024], FP16, kind="Internal").ap()
    dt_bf = nc.dram_tensor("dt_bf", [16, 128, 2048], FP16, kind="Internal").ap()
    out_bf = nc.dram_tensor("out_bf", [8, 128, 2048], FP16, kind="Internal").ap()
    up_bf = nc.dram_tensor("up_bf", [NEXP, 2, 128, 2048], FP16, kind="Internal").ap()
    dn_bf = nc.dram_tensor("dn_bf", [8, NGRP, 128, 2048], FP16, kind="Internal").ap()
    bcd_bfd = nc.dram_tensor("bcd_bfd", [128, 1280], FP16, kind="Internal").ap()
    gate_bfd = nc.dram_tensor("gate_bfd", [128, 128], FP16, kind="Internal").ap()

    dbg = {}
    if debug:
        def dbgt(name, shape):
            dbg[name] = nc.dram_tensor("dbg_" + name, shape, F32, kind="ExternalOutput").ap()
        dbgt("h1", [128, NCH_D * TT])
        dbgt("xc", [128, NCH_E * XMC])
        dbgt("ld", [1, TT])
        dbgt("scan", [16, TT])
        dbgt("ao", [128, NCH_D * TT])
        dbgt("x2", [128, NCH_D * TT])
        dbgt("h2", [128, NCH_D * TT])
        dbgt("scores", [128, 64])
        dbgt("combine", [16, TT])
        dbgt("ffn", [128, NCH_D * TT])

    P = Plan()

    with ExitStack() as ctx:
        sb = lambda name, shape, dt=F32: ctx.enter_context(nc.sbuf_tensor(name, shape, dt))
        ps = lambda name: ctx.enter_context(nc.psum_tensor(name, [128, 512], F32))

        c_eye = sb("c_eye", [128, 128])
        c_ones_bf = sb("c_ones_bf", [1, 128], FP16)
        c_ones1x16 = sb("c_ones1x16", [1, 16])
        c_ones16 = sb("c_ones16", [16, 1])
        c_ones1x128 = sb("c_ones1x128", [1, 128])
        c_inv1024 = sb("c_inv1024", [128, 1], FP16)
        c_inv2048 = sb("c_inv2048", [128, 1], FP16)
        c_in_b = sb("c_in_b", [128, 32])
        c_dt_b = sb("c_dt_b", [128, 16])
        c_out_b = sb("c_out_b", [128, 8])
        c_conv_w = sb("c_conv_w", [128, 48])
        c_conv_b = sb("c_conv_b", [128, 16])
        c_bcd_b = sb("c_bcd_b", [16, 3])
        c_bcd_bf = sb("c_bcd_bf", [128, 1280], FP16)
        c_gate_bf = sb("c_gate_bf", [128, 128], FP16)
        c_bias_e_tm = sb("c_bias_e_tm", [128, 64])
        c_betm_row = sb("c_betm_row", [1, 64])
        c_eps = sb("c_eps", [128, 1])
        c_one = sb("c_one", [128, 1])

        xlnt = [sb("xln0", [128, NCH_D * TT], FP16), sb("xln1", [128, NCH_D * TT], FP16)]
        h1t = [sb("h1a", [128, NCH_D * TT], FP16), sb("h1b", [128, NCH_D * TT], FP16)]
        h2 = sb("h2", [128, NCH_D * TT], FP16)
        xmt = [sb("xm0", [128, NCH_E * XMC], FP16), sb("xm1", [128, NCH_E * XMC], FP16)]
        sg = sb("sg", [128, NCH_E * TT], FP16)
        dtb = sb("dtb", [128, 2 * TT], FP16)
        sq = sb("sq", [128, 2 * TT], FP16)
        etmp = sb("etmp", [128, 4 * TT])
        aob = sb("aob", [128, NCH_D * TT], FP16)
        x2 = sb("x2", [128, NCH_D * TT], FP16)
        zs = sb("zs", [128, 16 * TT], FP16)
        ostage = sb("ostage", [128, 2 * TT])
        ptmp16 = sb("ptmp16", [128, 2 * TT], FP16)

        u16 = sb("u16", [16, TT])
        tmp16 = sb("tmp16", [16, TT])
        ct16 = sb("ct16", [16, TT])
        scano = sb("scano", [16, TT])
        s0 = sb("s0", [16, 1])
        rowA = sb("rowA", [1, TT])
        rowB = sb("rowB", [1, TT])
        rowAb = sb("rowAb", [1, TT], FP16)
        rowBb = sb("rowBb", [1, TT], FP16)
        lg16 = sb("lg16", [16, TT])
        sc_tm = sb("sc_tm", [128, 64])
        scor_tm = sb("scor_tm", [128, 64])
        gs = sb("gs", [128, 16])
        gpair = sb("gpair", [128, 16])
        thr2 = sb("thr2", [128, 4])
        gmask = sb("gmask", [128, 16])
        maskd = sb("maskd", [128, 64])
        maskw = sb("maskw", [128, 64])
        cmpb = sb("cmpb", [128, 16])
        mxc = sb("mxc", [128, 4])
        selb = sb("selb", [128, 64])
        wtm = sb("wtm", [128, 64])
        den = sb("den", [128, 4])
        com_tm = sb("com_tm", [128, 64])
        com_em = sb("com_em", [16, TT], FP16)
        cwrow = sb("cwrow", [1, 5 * TT], FP16)

        rstd_sb = sb("rstd_sb", [128, TT], FP16)
        mrstd_sb = sb("mrstd_sb", [128, TT], FP16)
        ys_sb = sb("ys_sb", [128, TT], FP16)
        cw_sb = sb("cw_sb", [128, 2 * TT], FP16)
        etmp16 = sb("etmp16", [128, 2 * TT], FP16)
        rw = sb("rw", [128, 5 * 2048], FP16)
        rdn = sb("rdn", [128, 3 * 2048], FP16)

        pb = [ps(f"pb{i}") for i in range(8)]
        def make_sem(k):
            return ctx.enter_context(nc.semaphore("sem_" + k))

        B_pb = [Buf() for _ in range(8)]
        B_xln = [[Buf() for _ in range(NCH_D)] for _ in range(2)]
        B_h1 = [[Buf() for _ in range(NCH_D)] for _ in range(2)]
        B_h2 = [Buf() for _ in range(NCH_D)]
        B_xm = [[Buf() for _ in range(NCH_E)] for _ in range(2)]
        B_sg = [Buf() for _ in range(NCH_E)]
        B_dtb = [Buf() for _ in range(2)]
        B_sq = [Buf() for _ in range(2)]
        B_etmp = [Buf() for _ in range(4)]
        B_aob = [Buf() for _ in range(NCH_D)]
        B_x2 = [Buf() for _ in range(NCH_D)]
        B_zs = [Buf() for _ in range(16)]
        B_ostage = [Buf() for _ in range(2)]
        B_pt16 = [Buf(), Buf()]
        B_cwrow = [Buf() for _ in range(5)]
        B_rw = [Buf() for _ in range(5)]
        B_rdn = [Buf() for _ in range(3)]
        B_bc1 = Buf()
        B_bc2 = Buf()
        B_ys = Buf()
        B_cw = [Buf(), Buf()]
        B_et16 = [Buf(), Buf()]
        B_const = Buf()
        B_c_bcd = Buf()
        B_c_gate = Buf()
        B_c_betm = Buf()
        B_small = {nm: Buf() for nm in
                   ("u16", "tmp16", "ct16", "scano", "s0", "rowA", "rowB", "rowAb",
                    "rowBb", "lg16", "sc_tm", "scor_tm", "gs", "gpair", "thr2",
                    "gmask", "maskd", "maskw", "cmpb", "mxc", "selb", "wtm", "den",
                    "com_tm", "com_em")}
        B_cast = {"misc": Buf()}
        B_scr = {"in": [Buf() for _ in range(16)], "dt": [Buf() for _ in range(16)],
                 "out": [Buf() for _ in range(8)], "up": [[Buf(), Buf()] for _ in range(NEXP)],
                 "dn": [[Buf() for _ in range(NGRP)] for _ in range(8)]}

        def op(eng, fn, reads=(), writes=(), deps=(), dma=False, sem=None):
            return P.op(eng, fn, reads, writes, deps, dma, sem)

        def mm_group(bank, parts, out_sub=None):
            bb = B_pb[bank]
            war = list(bb.w) + list(bb.r)
            n = len(parts)
            last = None
            oa = pb[bank][:, :] if out_sub is None else out_sub
            for i, (lf, rf, reads) in enumerate(parts):
                deps = war if i == 0 else []
                rdeps = []
                for b in reads:
                    rdeps += b.w
                iid = P.op("pe",
                           (lambda e, lf=lf, rf=rf, st=(i == 0), sp=(i == n - 1), oa=oa:
                            e.matmul(oa, lf(), rf(), start=st, stop=sp)),
                           deps=deps + rdeps)
                for b in reads:
                    b.r.append(iid)
                last = iid
            bb.w = [last]
            bb.r = []

        for jj in range(min(nt, 2)):
            xl0 = xlnt[jj % 2]
            op("pool", lambda e, jj=jj, xl0=xl0: e.dma_start(
                xl0[:, :].rearrange("p (c t) -> p c t", c=NCH_D),
                xT[:, TT * jj:TT * (jj + 1)].rearrange("(c p) t -> p c t", p=128)),
                writes=B_xln[jj % 2], dma=True, sem=f"xln{jj % 2}")
        op("pool", lambda e: e.dma_start(bcd_bfd[:, :], bcd_w[:, :]),
           writes=[B_cast["misc"]], dma=True, sem="c_misc")
        op("pool", lambda e: e.dma_start(gate_bfd[:, :], gate_wl[:, :]),
           writes=[B_cast["misc"]], dma=True, sem="c_misc")



        for fn in (
            lambda e: e.dma_start(c_eye[:, :], eye[:, :]),
            lambda e: e.dma_start(c_in_b[:, :], in_bl[:, :]),
            lambda e: e.dma_start(c_dt_b[:, :], dt_bl[:, :]),
            lambda e: e.dma_start(c_out_b[:, :], out_bl[:, :]),
            lambda e: e.dma_start(c_conv_w[:, :], conv_wl[:, :]),
            lambda e: e.dma_start(c_conv_b[:, :], conv_bl[:, :]),
            lambda e: e.dma_start(c_bcd_b[:, :], bcd_b[:, :]),
            lambda e: e.dma_start(c_betm_row[:, :], bias_e_tm[:, :]),
        ):
            op("sp", fn, writes=[B_const], dma=True, sem="spc")
        op("sp", lambda e: e.dma_start(c_bcd_bf[:, :], bcd_bfd[:, :]),
           reads=[B_cast["misc"]], writes=[B_c_bcd], dma=True, sem="spc")
        op("sp", lambda e: e.dma_start(c_gate_bf[:, :], gate_bfd[:, :]),
           reads=[B_cast["misc"]], writes=[B_c_gate], dma=True, sem="spc")
        for v, ap in ((1.0, c_ones_bf), (1.0, c_ones1x16), (1.0, c_ones16),
                      (1.0, c_ones1x128), (1.0 / 1024.0, c_inv1024), (1.0 / 2048.0, c_inv2048),
                      (EPS, c_eps), (1.0, c_one)):
            op("dve", lambda e, v=v, ap=ap: e.memset(ap[:, :], v), writes=[B_const])
        op("dve", lambda e: e.memset(s0[:, :], 0.0), writes=[B_small["s0"]])
        mm_group(5, [((lambda: c_ones1x128[0:1, :]), (lambda: c_betm_row[0:1, :]), [B_const])],
                 out_sub=pb[5][:, 0:64])
        op("dve", lambda e: e.tensor_copy(c_bias_e_tm[:, :], pb[5][:, 0:64]),
           reads=[B_pb[5]], writes=[B_c_betm])
        for k in range(NCH_E):
            op("dve", lambda e, k=k: e.memset(xmt[0][:, XMC * k:XMC * k + 1], 0.0),
               writes=[B_xm[0][k]])

        ring_slot = [0]
        def next_rw():
            s = ring_slot[0] % 5
            ring_slot[0] += 1
            return s
        rdn_slot = [0]
        def next_rdn():
            s = rdn_slot[0] % 3
            rdn_slot[0] += 1
            return s

        def wload(j, ap_f32, ap_scr, scr_buf, three_d=False):
            s = next_rw()
            dst = (rw[:, 2048 * s:2048 * (s + 1)].rearrange("p (E x) -> p E x", E=2)
                   if three_d else rw[:, 2048 * s:2048 * (s + 1)])
            if j == 0:
                op("pool", lambda e, dst=dst, a=ap_f32: e.dma_start(dst, a),
                   writes=[B_rw[s]], dma=True, sem=f"rwc{s}")
                op("sp", lambda e, dst=dst, a=ap_scr: e.dma_start(a, dst),
                   reads=[B_rw[s]], writes=[scr_buf], dma=True, sem="wb")
            else:
                op("sp", lambda e, dst=dst, a=ap_scr: e.dma_start(dst, a),
                   reads=[scr_buf], writes=[B_rw[s]], dma=True, sem=f"rw{s}")
            return s

        def load_x(j):
            xl = xlnt[j % 2]
            op("pool", lambda e, j=j, xl=xl: e.dma_start(
                xl[:, :].rearrange("p (c t) -> p c t", c=NCH_D),
                xT[:, TT * j:TT * (j + 1)].rearrange("(c p) t -> p c t", p=128)),
                writes=B_xln[j % 2], dma=True, sem=f"xln{j % 2}")

        def ln_stats_inline(src_bufs, src_ap, nch, f32src, sb0=0, sb1=1):
            assert not f32src
            war0 = list(B_pb[sb0].w) + list(B_pb[sb0].r)
            war1 = list(B_pb[sb1].w) + list(B_pb[sb1].r)
            last0 = last1 = None

            def stat_mms(c):
                nonlocal last0, last1
                cs = c % 2
                d0 = war0 if c == 0 else []
                last0 = P.op("pe", (lambda e, c=c, st=(c == 0), sp=(c == nch - 1), sb0=sb0:
                                    e.matmul(pb[sb0][0:1, :], c_inv1024[:, :], src_ap(c),
                                             start=st, stop=sp)),
                             deps=d0 + src_bufs[c].w + B_const.w)
                src_bufs[c].r.append(last0)
                d1 = war1 if c == 0 else []
                last1 = P.op("pe", (lambda e, cs=cs, st=(c == 0), sp=(c == nch - 1), sb1=sb1:
                                    e.matmul(pb[sb1][0:1, :], c_inv1024[:, :],
                                             sq[:, TT * cs:TT * (cs + 1)], start=st, stop=sp)),
                             deps=d1 + B_sq[cs].w + B_const.w)
                B_sq[cs].r.append(last1)

            for c in range(nch):
                cs = c % 2
                if c >= 2:
                    stat_mms(c - 2)
                op("dve", lambda e, c=c, cs=cs: e.tensor_mul(
                    sq[:, TT * cs:TT * (cs + 1)], src_ap(c), src_ap(c)),
                    reads=[src_bufs[c]], writes=[B_sq[cs]])
            stat_mms(nch - 2)
            stat_mms(nch - 1)
            B_pb[sb0].w = [last0]
            B_pb[sb0].r = []
            B_pb[sb1].w = [last1]
            B_pb[sb1].r = []

        def ao_stats(j):
            ln_stats_inline(B_aob, lambda c: aob[:, TT * c:TT * (c + 1)], NCH_D, False)

        def ln_rows(sb0=0, sb1=1):
            op("act", lambda e, sb0=sb0: e.activation(rowA[:, :], pb[sb0][0:1, :], AF.Square),
               reads=[B_pb[sb0]], writes=[B_small["rowA"]])
            op("dve", lambda e, sb1=sb1: e.tensor_sub(rowB[:, :], pb[sb1][0:1, :], rowA[:, :]),
               reads=[B_pb[sb1], B_small["rowA"]], writes=[B_small["rowB"]])
            op("act", lambda e: e.activation(rowB[:, :], rowB[:, :], AF.Ln, bias=c_eps[0:1, :]),
               reads=[B_small["rowB"], B_const], writes=[B_small["rowB"]])
            op("act", lambda e: e.activation(rowB[:, :], rowB[:, :], AF.Exp, scale=-0.5),
               reads=[B_small["rowB"]], writes=[B_small["rowB"]])
            op("dve", lambda e, sb0=sb0: e.tensor_mul(rowA[:, :], rowB[:, :], pb[sb0][0:1, :]),
               reads=[B_small["rowB"], B_pb[sb0]], writes=[B_small["rowA"]])
            op("dve", lambda e: e.tensor_copy(rowBb[:, :], rowB[:, :]),
               reads=[B_small["rowB"]], writes=[B_small["rowBb"]])
            op("dve", lambda e: e.tensor_copy(rowAb[:, :], rowA[:, :]),
               reads=[B_small["rowA"]], writes=[B_small["rowAb"]])
            mm_group(4, [((lambda: c_ones_bf[0:1, :]), (lambda: rowBb[0:1, :]),
                          [B_small["rowBb"], B_const])])
            mm_group(5, [((lambda: c_ones_bf[0:1, :]), (lambda: rowAb[0:1, :]),
                          [B_small["rowAb"], B_const])])
            op("dve", lambda e: e.tensor_copy(rstd_sb[:, :], pb[4][:, :]),
               reads=[B_pb[4]], writes=[B_bc1])
            op("dve", lambda e: e.tensor_copy(mrstd_sb[:, :], pb[5][:, :]),
               reads=[B_pb[5]], writes=[B_bc2])

        def ln1_stats(j):
            xl = xlnt[j % 2]
            ln_stats_inline(B_xln[j % 2], lambda c: xl[:, TT * c:TT * (c + 1)], NCH_D, False)

        def ln1_main(j):
            xl = xlnt[j % 2]
            ln_rows()
            h1 = h1t[j % 2]
            bh1 = B_h1[j % 2]
            for w0 in range(0, NCH_D, 4):
                for c in range(w0, w0 + 4):
                    et = c % 4
                    eng_ = "dve" if c % 2 == 0 else "pool"
                    op(eng_, lambda e, c=c, et=et, xl=xl: e.tensor_mul(
                        etmp[:, TT * et:TT * (et + 1)], xl[:, TT * c:TT * (c + 1)], rstd_sb[:, :]),
                        reads=[B_xln[j % 2][c], B_bc1], writes=[B_etmp[et]])
                for c in range(w0, w0 + 4):
                    et = c % 4
                    eng_ = "dve" if c % 2 == 0 else "pool"
                    op(eng_, lambda e, c=c, et=et: e.tensor_sub(
                        h1[:, TT * c:TT * (c + 1)], etmp[:, TT * et:TT * (et + 1)], mrstd_sb[:, :]),
                        reads=[B_etmp[et], B_bc2], writes=[bh1[c]])
            xm = xmt[j % 2]
            bxm = B_xm[j % 2]
            for pc in range(8):
                slot = wload(j, in_wt[2 * pc:2 * pc + 2].rearrange("E p x -> p E x"),
                             in_bf[2 * pc:2 * pc + 2].rearrange("E p x -> p E x"),
                             B_scr["in"][pc], three_d=True)
                for h in range(2):
                    k = 2 * pc + h
                    bank = (2, 3, 6, 7)[k % 4]
                    parts = []
                    for c in range(NCH_D):
                        parts.append((
                            (lambda s=slot, h=h, c=c:
                             rw[:, 2048 * s + 1024 * h + 128 * c:2048 * s + 1024 * h + 128 * (c + 1)]),
                            (lambda c=c, h1=h1: h1[:, TT * c:TT * (c + 1)]),
                            [B_rw[slot], bh1[c]]))
                    mm_group(bank, parts)
                    op("dve", lambda e, k=k, bank=bank, xm=xm: e.tensor_scalar(
                        xm[:, XMC * k + 1:XMC * k + 513], pb[bank][:, :],
                        c_in_b[:, k:k + 1], None, ALU.add),
                        reads=[B_pb[bank], B_const], writes=[bxm[k]])

        def halos(j):
            a, b = xmt[j % 2], xmt[(j + 1) % 2]
            ba, bb = B_xm[j % 2], B_xm[(j + 1) % 2]
            for k in range(NCH_E):
                op("dve", lambda e, k=k, a=a, b=b: e.tensor_copy(
                    b[:, XMC * k:XMC * k + 1], a[:, XMC * k + 512:XMC * k + 513]),
                    reads=[ba[k]], writes=[bb[k]])
                op("dve", lambda e, k=k, a=a, b=b: e.tensor_copy(
                    a[:, XMC * k + 513:XMC * k + 514], b[:, XMC * k + 1:XMC * k + 2]),
                    reads=[bb[k]], writes=[ba[k]])

        def right_edge(j):
            a = xmt[j % 2]
            ba = B_xm[j % 2]
            for k in range(NCH_E):
                op("dve", lambda e, k=k, a=a: e.memset(a[:, XMC * k + 513:XMC * k + 514], 0.0),
                   writes=[ba[k]])

        def conv(j):
            a = xmt[j % 2]
            ba = B_xm[j % 2]
            for k in range(NCH_E):
                et = k % 2
                ea = etmp16[:, TT * et:TT * (et + 1)]
                op("dve", lambda e, k=k, a=a, ea=ea: e.tensor_scalar(
                    ea, a[:, XMC * k:XMC * k + 512], c_conv_w[:, 3 * k:3 * k + 1], None, ALU.mult),
                    reads=[ba[k], B_const], writes=[B_et16[et]])
                op("dve", lambda e, k=k, a=a, ea=ea: e.scalar_tensor_tensor(
                    ea, a[:, XMC * k + 1:XMC * k + 513], c_conv_w[:, 3 * k + 1:3 * k + 2], ea,
                    ALU.mult, ALU.add),
                    reads=[ba[k], B_const, B_et16[et]], writes=[B_et16[et]])
                op("dve", lambda e, k=k, a=a, ea=ea: e.scalar_tensor_tensor(
                    ea, a[:, XMC * k + 2:XMC * k + 514], c_conv_w[:, 3 * k + 2:3 * k + 3], ea,
                    ALU.mult, ALU.add),
                    reads=[ba[k], B_const, B_et16[et]], writes=[B_et16[et]])
                op("act", lambda e, k=k, a=a, ea=ea: e.activation(
                    a[:, XMC * k + 1:XMC * k + 513], ea, AF.Silu, bias=c_conv_b[:, k:k + 1]),
                    reads=[B_et16[et], B_const], writes=[ba[k]])

        def xc_ap(j, c):
            return xmt[j % 2][:, XMC * c + 1:XMC * c + 513]

        def dt_phase(j):
            bxm = B_xm[j % 2]
            war0 = list(B_pb[0].w) + list(B_pb[0].r)
            last0 = None

            def dt_sum_mm(O):
                nonlocal last0
                d0 = war0 if O == 0 else []
                last0 = P.op("pe", (lambda e, O=O, st=(O == 0), sp=(O == 15):
                                    e.matmul(pb[0][0:1, :], c_inv2048[:, :],
                                             dtb[:, TT * (O % 2):TT * (O % 2 + 1)],
                                             start=st, stop=sp)),
                             deps=d0 + B_dtb[O % 2].w + B_const.w)
                B_dtb[O % 2].r.append(last0)

            for O in range(16):
                slot = wload(j, dt_wt[O], dt_bf[O], B_scr["dt"][O])
                bank = (2, 3, 6, 7)[O % 4]
                parts = []
                for c in range(NCH_E):
                    parts.append((
                        (lambda s=slot, c=c: rw[:, 2048 * s + 128 * c:2048 * s + 128 * (c + 1)]),
                        (lambda c=c, j=j: xc_ap(j, c)),
                        [B_rw[slot], bxm[c]]))
                mm_group(bank, parts)
                if O >= 2:
                    dt_sum_mm(O - 2)
                et = 2 + (O % 2)
                ea = etmp[:, TT * et:TT * (et + 1)]
                op("act", lambda e, O=O, bank=bank, ea=ea: e.activation(
                    ea, pb[bank][:, :], AF.Exp, bias=c_dt_b[:, O:O + 1]),
                    reads=[B_pb[bank], B_const], writes=[B_etmp[et]])
                op("act", lambda e, O=O, ea=ea: e.activation(
                    dtb[:, TT * (O % 2):TT * (O % 2 + 1)], ea, AF.Ln, bias=c_one[:, :]),
                    reads=[B_etmp[et], B_const], writes=[B_dtb[O % 2]])
            dt_sum_mm(14)
            dt_sum_mm(15)
            B_pb[0].w = [last0]
            B_pb[0].r = []

        def bcd_scan(j):
            bxm = B_xm[j % 2]

            parts = []
            for c in range(NCH_E):
                parts.append((
                    (lambda c=c: c_bcd_bf[:, 80 * c:80 * (c + 1)]),
                    (lambda c=c, j=j: xc_ap(j, c)),
                    [B_c_bcd, bxm[c]]))
            mm_group(6, parts, out_sub=pb[6][0:80, :])
            op("dve", lambda e: e.tensor_scalar(
                u16[:, :], pb[6][0:16, :], c_bcd_b[:, 0:1], None, ALU.add),
                reads=[B_pb[6], B_const], writes=[B_small["u16"]])
            op("dve", lambda e: e.tensor_scalar(
                tmp16[:, :], pb[6][64:80, :], c_bcd_b[:, 2:3], None, ALU.add),
                reads=[B_pb[6], B_const], writes=[B_small["tmp16"]])
            op("dve", lambda e: e.tensor_mul(u16[:, :], u16[:, :], tmp16[:, :]),
               reads=[B_small["u16"], B_small["tmp16"]], writes=[B_small["u16"]])
            op("dve", lambda e: e.tensor_scalar(
                ct16[:, :], pb[6][32:48, :], c_bcd_b[:, 1:2], None, ALU.add),
                reads=[B_pb[6], B_const], writes=[B_small["ct16"]])
            op("dve", lambda e: e.tensor_scalar(rowA[:, :], pb[0][0:1, :], 10.0, -1.0,
                                                ALU.min, ALU.mult),
               reads=[B_pb[0]], writes=[B_small["rowA"]])
            if debug and j == 0:
                op("pool", lambda e: e.dma_start(dbg["ld"][:, :], rowA[:, :]),
                   reads=[B_small["rowA"]], dma=True, sem="dbgq")
            op("act", lambda e: e.activation(rowB[:, :], rowA[:, :], AF.Exp),
               reads=[B_small["rowA"]], writes=[B_small["rowB"]])
            mm_group(6, [((lambda: c_ones1x16[0:1, :]), (lambda: rowB[0:1, :]),
                          [B_small["rowB"], B_const])], out_sub=pb[6][0:16, :])
            op("dve", lambda e: e.tensor_tensor_scan(scano[:, :], pb[6][0:16, :], u16[:, :],
                                                     s0[:, :], ALU.mult, ALU.add),
               reads=[B_pb[6], B_small["u16"], B_small["s0"]], writes=[B_small["scano"]])
            op("dve", lambda e: e.tensor_copy(s0[:, :], scano[:, 511:512]),
               reads=[B_small["scano"]], writes=[B_small["s0"]])
            if debug and j == 0:
                op("pool", lambda e: e.dma_start(dbg["scan"][:, :], scano[:, :]),
                   reads=[B_small["scano"]], dma=True, sem="dbgq")
            op("dve", lambda e: e.tensor_mul(ct16[:, :], scano[:, :], ct16[:, :]),
               reads=[B_small["scano"], B_small["ct16"]], writes=[B_small["ct16"]])
            mm_group(0, [((lambda: c_ones16[:, :]), (lambda: ct16[:, :]),
                          [B_small["ct16"], B_const])], out_sub=pb[0][0:1, :])
            op("dve", lambda e: e.tensor_copy(rowAb[:, :], pb[0][0:1, :]),
               reads=[B_pb[0]], writes=[B_small["rowAb"]])
            mm_group(4, [((lambda: c_ones_bf[0:1, :]), (lambda: rowAb[0:1, :]),
                          [B_small["rowAb"], B_const])])

        def outproj_round(j, r):
            banks = (6, 7) if r % 2 == 0 else (2, 3)
            slots = [wload(j, out_wt[2 * r + q], out_bf[2 * r + q], B_scr["out"][2 * r + q])
                     for q in range(2)]
            wars = {q: list(B_pb[banks[q]].w) + list(B_pb[banks[q]].r) for q in range(2)}
            lasts = {}
            for k in range(NCH_E):
                for q in range(2):
                    reads = [B_rw[slots[q]], B_sg[k]]
                    deps = wars[q] if k == 0 else []
                    rdeps = []
                    for b in reads:
                        rdeps += b.w
                    iid = P.op("pe", (lambda e, sl=slots[q], k=k,
                                      st=(k == 0), sp=(k == NCH_E - 1), bank=banks[q]:
                                      e.matmul(pb[bank][:, :],
                                               rw[:, 2048 * sl + 128 * k:2048 * sl + 128 * (k + 1)],
                                               sg[:, TT * k:TT * (k + 1)],
                                               start=st, stop=sp)),
                               deps=deps + rdeps)
                    for b in reads:
                        b.r.append(iid)
                    lasts[q] = iid
            for q in range(2):
                bank = banks[q]
                B_pb[bank].w = [lasts[q]]
                B_pb[bank].r = []
                oc = 2 * r + q
                op("dve", lambda e, oc=oc, bank=bank: e.tensor_copy(
                    zs[:, TT * oc:TT * (oc + 1)], pb[bank][:, :]),
                    reads=[B_pb[bank]], writes=[B_zs[oc]])

        def ao_finalize(j):
            h1 = h1t[j % 2]
            bh1 = B_h1[j % 2]
            op("dve", lambda e: e.tensor_copy(ys_sb[:, :], pb[4][:, :]),
               reads=[B_pb[4]], writes=[B_ys])
            for w0 in range(0, NCH_D, 4):
                for oc in range(w0, w0 + 4):
                    et = oc % 4
                    op("dve", lambda e, oc=oc, et=et: e.tensor_mul(
                        etmp[:, TT * et:TT * (et + 1)], zs[:, TT * oc:TT * (oc + 1)], ys_sb[:, :]),
                        reads=[B_zs[oc], B_ys], writes=[B_etmp[et]])
                for oc in range(w0, w0 + 4):
                    et = oc % 4
                    op("dve", lambda e, oc=oc, et=et, h1=h1: e.scalar_tensor_tensor(
                        aob[:, TT * oc:TT * (oc + 1)], etmp[:, TT * et:TT * (et + 1)],
                        c_out_b[:, oc:oc + 1], h1[:, TT * oc:TT * (oc + 1)], ALU.add, ALU.add),
                        reads=[B_etmp[et], B_const, bh1[oc]], writes=[B_aob[oc]])
            if debug and j == 0:
                for oc in range(NCH_D):
                    op("pool", lambda e, oc=oc: e.dma_start(
                        dbg["ao"][:, TT * oc:TT * (oc + 1)], aob[:, TT * oc:TT * (oc + 1)]),
                        reads=[B_aob[oc]], dma=True, sem="dbgq")

        def ln3_residual(j):
            ln_rows()
            xl = xlnt[j % 2]
            for w0 in range(0, NCH_D, 4):
                for oc in range(w0, w0 + 4):
                    et = oc % 4
                    eng_ = "dve" if oc % 2 == 0 else "pool"
                    op(eng_, lambda e, oc=oc, et=et: e.tensor_mul(
                        etmp[:, TT * et:TT * (et + 1)], aob[:, TT * oc:TT * (oc + 1)], rstd_sb[:, :]),
                        reads=[B_aob[oc], B_bc1], writes=[B_etmp[et]])
                for oc in range(w0, w0 + 4):
                    et = oc % 4
                    eng_ = "dve" if oc % 2 == 0 else "pool"
                    op(eng_, lambda e, et=et: e.tensor_sub(
                        etmp[:, TT * et:TT * (et + 1)], etmp[:, TT * et:TT * (et + 1)], mrstd_sb[:, :]),
                        reads=[B_etmp[et], B_bc2], writes=[B_etmp[et]])
                for oc in range(w0, w0 + 4):
                    et = oc % 4
                    eng_ = "pool" if oc % 2 == 0 else "dve"
                    op(eng_, lambda e, oc=oc, et=et, xl=xl: e.tensor_add(
                        x2[:, TT * oc:TT * (oc + 1)], etmp[:, TT * et:TT * (et + 1)],
                        xl[:, TT * oc:TT * (oc + 1)]),
                        reads=[B_etmp[et], B_xln[j % 2][oc]], writes=[B_x2[oc]])
            if debug and j == 0:
                for oc in range(NCH_D):
                    op("pool", lambda e, oc=oc: e.dma_start(
                        dbg["x2"][:, TT * oc:TT * (oc + 1)], x2[:, TT * oc:TT * (oc + 1)]),
                        reads=[B_x2[oc]], dma=True, sem="dbgq")

        def ln4(j):
            ln_stats_inline(B_x2, lambda c: x2[:, TT * c:TT * (c + 1)], NCH_D, False)
            ln_rows()
            for w0 in range(0, NCH_D, 4):
                for c in range(w0, w0 + 4):
                    et = c % 4
                    eng_ = "dve" if c % 2 == 0 else "pool"
                    op(eng_, lambda e, c=c, et=et: e.tensor_mul(
                        etmp[:, TT * et:TT * (et + 1)], x2[:, TT * c:TT * (c + 1)], rstd_sb[:, :]),
                        reads=[B_x2[c], B_bc1], writes=[B_etmp[et]])
                for c in range(w0, w0 + 4):
                    et = c % 4
                    eng_ = "dve" if c % 2 == 0 else "pool"
                    op(eng_, lambda e, c=c, et=et: e.tensor_sub(
                        h2[:, TT * c:TT * (c + 1)], etmp[:, TT * et:TT * (et + 1)], mrstd_sb[:, :]),
                        reads=[B_etmp[et], B_bc2], writes=[B_h2[c]])
            if debug and j == 0:
                for c in range(NCH_D):
                    op("pool", lambda e, c=c: e.dma_start(
                        dbg["h2"][:, TT * c:TT * (c + 1)], h2[:, TT * c:TT * (c + 1)]),
                        reads=[B_h2[c]], dma=True, sem="dbgq")

        def inprojB(j, half=None):
            h1 = h1t[j % 2]
            bh1 = B_h1[j % 2]
            rng = range(8) if half is None else (range(4) if half == 0 else range(4, 8))
            for pc in rng:
                slot = wload(j, in_wt[16 + 2 * pc:16 + 2 * pc + 2].rearrange("E p x -> p E x"),
                             in_bf[16 + 2 * pc:16 + 2 * pc + 2].rearrange("E p x -> p E x"),
                             B_scr["in"][8 + pc], three_d=True)
                for h in range(2):
                    k = 2 * pc + h
                    bank = (2, 3, 6, 7)[k % 4]
                    parts = []
                    for c in range(NCH_D):
                        parts.append((
                            (lambda s=slot, h=h, c=c:
                             rw[:, 2048 * s + 1024 * h + 128 * c:2048 * s + 1024 * h + 128 * (c + 1)]),
                            (lambda c=c, h1=h1: h1[:, TT * c:TT * (c + 1)]),
                            [B_rw[slot], bh1[c]]))
                    mm_group(bank, parts)
                    op("act", lambda e, k=k, bank=bank: e.activation(
                        sg[:, TT * k:TT * (k + 1)], pb[bank][:, :], AF.Silu,
                        bias=c_in_b[:, 16 + k:17 + k]),
                        reads=[B_pb[bank], B_const], writes=[B_sg[k]])

        def router_logits(j):
            parts = []
            for c in range(NCH_D):
                parts.append((
                    (lambda c=c: c_gate_bf[:, 16 * c:16 * (c + 1)]),
                    (lambda c=c: h2[:, TT * c:TT * (c + 1)]),
                    [B_c_gate, B_h2[c]]))
            mm_group(6, parts, out_sub=pb[6][0:16, :])
            op("dve", lambda e: e.tensor_copy(lg16[:, :], pb[6][0:16, :]),
               reads=[B_pb[6]], writes=[B_small["lg16"]])
            tr_parts = []
            war5 = list(B_pb[5].w) + list(B_pb[5].r)
            last = None
            for c in range(4):
                deps = war5 if c == 0 else []
                last = P.op("pe", (lambda e, c=c: e.transpose(
                    pb[5][:, 16 * c:16 * (c + 1)], lg16[:, 128 * c:128 * (c + 1)],
                    c_eye[0:16, 0:16])),
                    deps=deps + B_small["lg16"].w + B_const.w)
                B_small["lg16"].r.append(last)
            B_pb[5].w = [last]
            B_pb[5].r = []

        def router_rest(j):
            op("act", lambda e: e.activation(scor_tm[:, :], pb[5][:, 0:64], AF.Exp, scale=-1.0),
               reads=[B_pb[5]], writes=[B_small["scor_tm"]])
            op("dve", lambda e: e.tensor_scalar(scor_tm[:, :], scor_tm[:, :], 1.0, None, ALU.add),
               reads=[B_small["scor_tm"]], writes=[B_small["scor_tm"]])
            op("dve", lambda e: e.reciprocal(scor_tm[:, :], scor_tm[:, :]),
               reads=[B_small["scor_tm"]], writes=[B_small["scor_tm"]])
            op("dve", lambda e: e.tensor_add(sc_tm[:, :], scor_tm[:, :], c_bias_e_tm[:, :]),
               reads=[B_small["scor_tm"], B_c_betm], writes=[B_small["sc_tm"]])
            if debug and j == 0:
                op("pool", lambda e: e.dma_start(dbg["scores"][:, :], scor_tm[:, :]),
                   reads=[B_small["scor_tm"]], dma=True, sem="dbgq")
            pairs = [(0, 1), (0, 2), (0, 3), (1, 2), (1, 3), (2, 3)]
            for n, (i, jj) in enumerate(pairs):
                if n == 0:
                    op("dve", lambda e, i=i, jj=jj: e.tensor_add(
                        gs[:, :], sc_tm[:, i::4], sc_tm[:, jj::4]),
                        reads=[B_small["sc_tm"]], writes=[B_small["gs"]])
                else:
                    op("dve", lambda e, i=i, jj=jj: e.tensor_add(
                        gpair[:, :], sc_tm[:, i::4], sc_tm[:, jj::4]),
                        reads=[B_small["sc_tm"]], writes=[B_small["gpair"]])
                    op("dve", lambda e: e.tensor_max(gs[:, :], gs[:, :], gpair[:, :]),
                       reads=[B_small["gs"], B_small["gpair"]], writes=[B_small["gs"]])
            for n, (i, jj) in enumerate(pairs):
                if n == 0:
                    op("dve", lambda e, i=i, jj=jj: e.tensor_tensor(
                        thr2[:, :], gs[:, i::4], gs[:, jj::4], ALU.min),
                        reads=[B_small["gs"]], writes=[B_small["thr2"]])
                else:
                    op("dve", lambda e, i=i, jj=jj: e.tensor_tensor(
                        mxc[:, :], gs[:, i::4], gs[:, jj::4], ALU.min),
                        reads=[B_small["gs"]], writes=[B_small["mxc"]])
                    op("dve", lambda e: e.tensor_max(thr2[:, :], thr2[:, :], mxc[:, :]),
                       reads=[B_small["thr2"], B_small["mxc"]], writes=[B_small["thr2"]])
            for c in range(4):
                op("dve", lambda e, c=c: e.tensor_scalar(
                    gmask[:, 4 * c:4 * (c + 1)], gs[:, 4 * c:4 * (c + 1)],
                    thr2[:, c:c + 1], None, ALU.is_ge),
                    reads=[B_small["gs"], B_small["thr2"]], writes=[B_small["gmask"]])
            for c in range(4):
                for g in range(4):
                    op("dve", lambda e, c=c, g=g: e.tensor_scalar(
                        maskd[:, 16 * c + 4 * g:16 * c + 4 * (g + 1)],
                        sc_tm[:, 16 * c + 4 * g:16 * c + 4 * (g + 1)],
                        gmask[:, 4 * c + g:4 * c + g + 1], None, ALU.mult),
                        reads=[B_small["sc_tm"], B_small["gmask"]], writes=[B_small["maskd"]])
            op("dve", lambda e: e.tensor_copy(maskw[:, :], maskd[:, :]),
               reads=[B_small["maskd"]], writes=[B_small["maskw"]])
            for r in range(3):
                for c in range(4):
                    op("dve", lambda e, c=c: e.reduce_max(
                        mxc[:, c:c + 1], maskw[:, 16 * c:16 * (c + 1)], axis=AX.X),
                        reads=[B_small["maskw"]], writes=[B_small["mxc"]])
                for c in range(4):
                    op("dve", lambda e, c=c: e.tensor_scalar(
                        cmpb[:, :], maskw[:, 16 * c:16 * (c + 1)], mxc[:, c:c + 1],
                        None, ALU.is_ge),
                        reads=[B_small["maskw"], B_small["mxc"]], writes=[B_small["cmpb"]])
                    op("dve", lambda e, c=c: e.scalar_tensor_tensor(
                        maskw[:, 16 * c:16 * (c + 1)], cmpb[:, :], -BIG,
                        maskw[:, 16 * c:16 * (c + 1)], ALU.mult, ALU.add),
                        reads=[B_small["cmpb"], B_small["maskw"]], writes=[B_small["maskw"]])
            for c in range(4):
                op("dve", lambda e, c=c: e.reduce_max(
                    mxc[:, c:c + 1], maskw[:, 16 * c:16 * (c + 1)], axis=AX.X),
                    reads=[B_small["maskw"]], writes=[B_small["mxc"]])
            for c in range(4):
                op("dve", lambda e, c=c: e.tensor_scalar(
                    selb[:, 16 * c:16 * (c + 1)], maskd[:, 16 * c:16 * (c + 1)],
                    mxc[:, c:c + 1], None, ALU.is_ge),
                    reads=[B_small["maskd"], B_small["mxc"]], writes=[B_small["selb"]])
            op("dve", lambda e: e.tensor_mul(wtm[:, :], scor_tm[:, :], selb[:, :]),
               reads=[B_small["scor_tm"], B_small["selb"]], writes=[B_small["wtm"]])
            for c in range(4):
                op("dve", lambda e, c=c: e.tensor_reduce(
                    den[:, c:c + 1], wtm[:, 16 * c:16 * (c + 1)], AX.X, ALU.add),
                    reads=[B_small["wtm"]], writes=[B_small["den"]])
            op("dve", lambda e: e.tensor_scalar(den[:, :], den[:, :], 1.0 / RSCALE,
                                                1e-20 / RSCALE, ALU.mult, ALU.add),
               reads=[B_small["den"]], writes=[B_small["den"]])
            op("dve", lambda e: e.reciprocal(den[:, :], den[:, :]),
               reads=[B_small["den"]], writes=[B_small["den"]])
            for c in range(4):
                op("dve", lambda e, c=c: e.tensor_scalar(
                    com_tm[:, 16 * c:16 * (c + 1)], wtm[:, 16 * c:16 * (c + 1)],
                    den[:, c:c + 1], None, ALU.mult),
                    reads=[B_small["wtm"], B_small["den"]], writes=[B_small["com_tm"]])
        def router_combine(j):
            war5 = list(B_pb[5].w) + list(B_pb[5].r)
            last = None
            for c in range(4):
                deps = war5 if c == 0 else []
                last = P.op("pe", (lambda e, c=c: e.transpose(
                    pb[5][0:16, 128 * c:128 * (c + 1)], com_tm[:, 16 * c:16 * (c + 1)],
                    c_eye[:, :])),
                    deps=deps + B_small["com_tm"].w + B_const.w)
                B_small["com_tm"].r.append(last)
            B_pb[5].w = [last]
            B_pb[5].r = []
            op("dve", lambda e: e.tensor_copy(com_em[:, :], pb[5][0:16, :]),
               reads=[B_pb[5]], writes=[B_small["com_em"]])
            if debug and j == 0:
                op("pool", lambda e: e.dma_start(dbg["combine"][:, :], com_em[:, :]),
                   reads=[B_small["com_em"]], dma=True, sem="dbgq")

        def cw_fetch_one(ridx):
            cs = ridx % 5
            op("act", lambda e, r=ridx, cs=cs: e.dma_start(
                cwrow[:, TT * cs:TT * (cs + 1)], com_em[r:r + 1, :]),
                reads=[B_small["com_em"]], writes=[B_cwrow[cs]], dma=True, sem=f"cw{cs}")

        def ffn_ups(j, g):
            if True:
                for eL in range(4):
                    es = 16 + eL if g == 0 else 4 * (g - 1) + eL
                    routed = g > 0
                    if routed:
                        ridx = 4 * (g - 1) + eL
                        if ridx + 2 < 16:
                            cw_fetch_one(ridx + 2)
                        cs = ridx % 5
                        cb = 4 + (eL % 2)
                        mm_group(cb, [((lambda: c_ones_bf[0:1, :]),
                                       (lambda cs=cs: cwrow[0:1, TT * cs:TT * (cs + 1)]),
                                       [B_cwrow[cs], B_const])])
                        op("dve", lambda e, cb=cb, cs2=eL % 2: e.tensor_copy(
                            cw_sb[:, TT * cs2:TT * (cs2 + 1)], pb[cb][:, :]),
                            reads=[B_pb[cb]], writes=[B_cw[eL % 2]])
                    for hh in range(2):
                        slot = wload(j, up_wt[es, hh], up_bf[es, hh], B_scr["up"][es][hh])
                        for ii in range(2):
                            ic = 2 * hh + ii
                            bank = (2, 3, 6, 7)[(4 * eL + ic) % 4]
                            parts = []
                            for c in range(NCH_D):
                                parts.append((
                                    (lambda s=slot, ii=ii, c=c:
                                     rw[:, 2048 * s + 1024 * ii + 128 * c:2048 * s + 1024 * ii + 128 * (c + 1)]),
                                    (lambda c=c: h2[:, TT * c:TT * (c + 1)]),
                                    [B_rw[slot], B_h2[c]]))
                            mm_group(bank, parts)
                            zi = 4 * eL + ic
                            et = ic % 2
                            ea = etmp16[:, TT * et:TT * (et + 1)]
                            op("act", lambda e, bank=bank, ea=ea: e.activation(
                                ea, pb[bank][:, :], AF.Relu),
                                reads=[B_pb[bank]], writes=[B_et16[et]])
                            if routed:
                                op("act", lambda e, ea=ea: e.activation(ea, ea, AF.Square),
                                   reads=[B_et16[et]], writes=[B_et16[et]])
                                op("dve", lambda e, zi=zi, ea=ea, cs2=eL % 2: e.tensor_mul(
                                    zs[:, TT * zi:TT * (zi + 1)], ea,
                                    cw_sb[:, TT * cs2:TT * (cs2 + 1)]),
                                    reads=[B_et16[et], B_cw[eL % 2]], writes=[B_zs[zi]])
                            else:
                                op("act", lambda e, zi=zi, ea=ea: e.activation(
                                    zs[:, TT * zi:TT * (zi + 1)], ea, AF.Square),
                                    reads=[B_et16[et]], writes=[B_zs[zi]])
        def ffn_downs(j, g):
            if True:
                for oc in range(8):
                    slot = next_rdn()
                    if j == 0:
                        op("pool", lambda e, s=slot, oc=oc, g=g: e.dma_start(
                            rdn[:, 2048 * s:2048 * (s + 1)], dn_wt[oc, g]),
                            writes=[B_rdn[slot]], dma=True, sem=f"rdnc{slot}")
                        op("sp", lambda e, s=slot, oc=oc, g=g: e.dma_start(
                            dn_bf[oc, g], rdn[:, 2048 * s:2048 * (s + 1)]),
                            reads=[B_rdn[slot]], writes=[B_scr["dn"][oc][g]], dma=True, sem="wb")
                    else:
                        op("sp", lambda e, s=slot, oc=oc, g=g: e.dma_start(
                            rdn[:, 2048 * s:2048 * (s + 1)], dn_bf[oc, g]),
                            reads=[B_scr["dn"][oc][g]], writes=[B_rdn[slot]], dma=True, sem=f"rdn{slot}")
                    bank = (6, 7, 2, 3)[oc % 4]
                    parts = []
                    for zi in range(16):
                        parts.append((
                            (lambda s=slot, zi=zi: rdn[:, 2048 * s + 128 * zi:2048 * s + 128 * (zi + 1)]),
                            (lambda zi=zi: zs[:, TT * zi:TT * (zi + 1)]),
                            [B_rdn[slot], B_zs[zi]]))
                    mm_group(bank, parts)
                    if g == 0:
                        op("dve", lambda e, oc=oc, bank=bank: e.tensor_copy(
                            aob[:, TT * oc:TT * (oc + 1)], pb[bank][:, :]),
                            reads=[B_pb[bank]], writes=[B_aob[oc]])
                    else:
                        op("dve", lambda e, oc=oc, bank=bank: e.tensor_add(
                            aob[:, TT * oc:TT * (oc + 1)], pb[bank][:, :],
                            aob[:, TT * oc:TT * (oc + 1)]),
                            reads=[B_pb[bank], B_aob[oc]], writes=[B_aob[oc]])
            if debug and j == 0 and g == NGRP - 1:
                for oc in range(NCH_D):
                    op("pool", lambda e, oc=oc: e.dma_start(
                        dbg["ffn"][:, TT * oc:TT * (oc + 1)], aob[:, TT * oc:TT * (oc + 1)]),
                        reads=[B_aob[oc]], dma=True, sem="dbgq")

        store_deps = []

        def ln5_stats(j):
            ln_stats_inline(B_aob, lambda c: aob[:, TT * c:TT * (c + 1)], NCH_D, False,
                            sb0=2, sb1=3)

        def ln5_apply_store(j):
            for w0 in range(0, NCH_D, 4):
                for oc in range(w0, w0 + 4):
                    et = oc % 4
                    eng_ = "dve" if oc % 2 == 0 else "pool"
                    op(eng_, lambda e, oc=oc, et=et: e.tensor_mul(
                        etmp[:, TT * et:TT * (et + 1)], aob[:, TT * oc:TT * (oc + 1)], rstd_sb[:, :]),
                        reads=[B_aob[oc], B_bc1], writes=[B_etmp[et]])
                for oc in range(w0, w0 + 4):
                    et = oc % 4
                    eng_ = "dve" if oc % 2 == 0 else "pool"
                    op(eng_, lambda e, et=et: e.tensor_sub(
                        etmp[:, TT * et:TT * (et + 1)], etmp[:, TT * et:TT * (et + 1)], mrstd_sb[:, :]),
                        reads=[B_etmp[et], B_bc2], writes=[B_etmp[et]])
                for oc in range(w0, w0 + 4):
                    et = oc % 4
                    os_ = oc % 2
                    eng_ = "pool" if oc % 2 == 0 else "dve"
                    op(eng_, lambda e, oc=oc, et=et, os_=os_: e.tensor_add(
                        ostage[:, TT * os_:TT * (os_ + 1)], etmp[:, TT * et:TT * (et + 1)],
                        x2[:, TT * oc:TT * (oc + 1)]),
                        reads=[B_etmp[et], B_x2[oc]], writes=[B_ostage[os_]])
                    sid = op("pool", lambda e, oc=oc, j=j, os_=os_: e.dma_start(
                        outT[128 * oc:128 * (oc + 1), TT * j:TT * (j + 1)],
                        ostage[:, TT * os_:TT * (os_ + 1)]),
                        reads=[B_ostage[os_]], dma=True, sem="st")
                    store_deps.append(sid)

        ln1_stats(0)
        ln1_main(0)
        inprojB(0)
        if nt > 1:
            ln1_stats(1)
        if debug:
            for c in range(NCH_D):
                op("pool", lambda e, c=c: e.dma_start(
                    dbg["h1"][:, TT * c:TT * (c + 1)], h1t[0][:, TT * c:TT * (c + 1)]),
                    reads=[B_h1[0][c]], dma=True, sem="dbgq")
        for j in range(nt):
            if j > 0:
                ln_rows(sb0=2, sb1=3)
                ln5_apply_store(j - 1)
            outproj_round(j, 0)
            outproj_round(j, 1)
            if j + 1 < nt:
                ln1_main(j + 1)
                halos(j)
            else:
                right_edge(j)
            conv(j)
            if debug and j == 0:
                for k in range(NCH_E):
                    op("pool", lambda e, k=k: e.dma_start(
                        dbg["xc"][:, XMC * k:XMC * k + 514], xmt[0][:, XMC * k:XMC * k + 514]),
                        reads=[B_xm[0][k]], dma=True, sem="dbgq")
            outproj_round(j, 2)
            dt_phase(j)
            bcd_scan(j)
            outproj_round(j, 3)
            if j + 1 < nt:
                inprojB(j + 1, half=0)
            ao_finalize(j)
            ao_stats(j)
            if j + 1 < nt:
                inprojB(j + 1, half=1)
            ln3_residual(j)
            if j + 2 < nt:
                load_x(j + 2)
            ln4(j)
            router_logits(j)
            ffn_ups(j, 0)
            router_rest(j)
            ffn_downs(j, 0)
            router_combine(j)
            cw_fetch_one(0)
            cw_fetch_one(1)
            for g in range(1, NGRP):
                ffn_ups(j, g)
                ffn_downs(j, g)
            ln5_stats(j)
            if j + 2 < nt:
                ln1_stats(j + 2)

        ln_rows(sb0=2, sb1=3)
        ln5_apply_store(nt - 1)

        all_dma_deps = [i for i, o in enumerate(P.ops) if o["dma"]]
        op("sp", lambda e: e.nop(), deps=store_deps + all_dma_deps)
        counts = P.finalize_and_emit(nc, make_sem)
    return nc, counts, dbg


def host_layout(in_w, in_b, conv_w, conv_b, dt_w, dt_b, B_w, B_b, C_w, C_b,
                D_w, D_b, out_w, out_b, gate_w, bias_e, up_w, down_w,
                shared_up, shared_down):
    f = np.ascontiguousarray
    d = {}
    d["in_wt"] = f(in_w.reshape(32, 128, 8, 128).transpose(0, 3, 2, 1).reshape(32, 128, 1024))
    d["dt_wt"] = f(dt_w.reshape(16, 128, 16, 128).transpose(0, 3, 2, 1).reshape(16, 128, 2048))
    d["out_wt"] = f(out_w.reshape(8, 128, 16, 128).transpose(0, 3, 2, 1).reshape(8, 128, 2048))
    upr = up_w.reshape(16, 4, 128, 8, 128).transpose(0, 1, 4, 3, 2).reshape(16, 4, 128, 1024)
    ups = shared_up.reshape(4, 4, 128, 8, 128).transpose(0, 1, 4, 3, 2).reshape(4, 4, 128, 1024)
    up_all = np.concatenate([upr, ups], axis=0)
    d["up_wt"] = f(up_all.reshape(20, 2, 2, 128, 1024).transpose(0, 1, 3, 2, 4)
                   .reshape(20, 2, 128, 2048))
    dnr = down_w.reshape(16, 8, 128, 4, 128).transpose(1, 0, 4, 3, 2).reshape(8, 16, 128, 512)
    dns = shared_down.reshape(8, 128, 4, 4, 128).transpose(0, 2, 4, 3, 1).reshape(8, 4, 128, 512)
    dn_all = np.concatenate([dns, dnr], axis=1)
    d["dn_wt"] = f(dn_all.reshape(8, 5, 4, 128, 512).transpose(0, 1, 3, 2, 4)
                   .reshape(8, 5, 128, 2048))
    bcd = np.zeros((128, 16, 80), np.float32)
    for i, w in enumerate((B_w, C_w, D_w)):
        bcd[:, :, 32 * i:32 * i + 16] = w.T.reshape(16, 128, 16).transpose(1, 0, 2)
    d["bcd_w"] = f(bcd.reshape(128, 1280))
    d["gate_wl"] = f(gate_w.T.reshape(8, 128, 16).transpose(1, 0, 2).reshape(128, 128))
    d["conv_wl"] = f(conv_w.reshape(16, 128, 3).transpose(1, 0, 2).reshape(128, 48))
    d["conv_bl"] = f(conv_b.reshape(16, 128).T)
    d["in_bl"] = f(in_b.reshape(32, 128).T)
    d["dt_bl"] = f(dt_b.reshape(16, 128).T)
    d["out_bl"] = f(out_b.reshape(8, 128).T)
    d["bcd_b"] = f(np.stack([B_b, C_b, D_b], axis=1))
    d["bias_e_tm"] = f(np.tile(bias_e[None, :], (1, 4)))
    d["eye"] = np.eye(128, dtype=np.float32)
    return d


_CACHED = {}


def kernel(L0_attn_x, L0_attn_in_w, L0_attn_in_b, L0_attn_conv_w, L0_attn_conv_b,
           L0_attn_dt_w, L0_attn_dt_b, L0_attn_B_w, L0_attn_B_b, L0_attn_C_w,
           L0_attn_C_b, L0_attn_D_w, L0_attn_D_b, L0_attn_out_w, L0_attn_out_b,
           L0_ffn_gate_w, L0_ffn_bias_e, L0_ffn_up_w, L0_ffn_down_w,
           L0_ffn_shared_up, L0_ffn_shared_down):
    x = np.asarray(L0_attn_x, dtype=np.float32)
    w = host_layout(np.asarray(L0_attn_in_w, np.float32), np.asarray(L0_attn_in_b, np.float32),
                    np.asarray(L0_attn_conv_w, np.float32), np.asarray(L0_attn_conv_b, np.float32),
                    np.asarray(L0_attn_dt_w, np.float32), np.asarray(L0_attn_dt_b, np.float32),
                    np.asarray(L0_attn_B_w, np.float32), np.asarray(L0_attn_B_b, np.float32),
                    np.asarray(L0_attn_C_w, np.float32), np.asarray(L0_attn_C_b, np.float32),
                    np.asarray(L0_attn_D_w, np.float32), np.asarray(L0_attn_D_b, np.float32),
                    np.asarray(L0_attn_out_w, np.float32), np.asarray(L0_attn_out_b, np.float32),
                    np.asarray(L0_ffn_gate_w, np.float32), np.asarray(L0_ffn_bias_e, np.float32),
                    np.asarray(L0_ffn_up_w, np.float32), np.asarray(L0_ffn_down_w, np.float32),
                    np.asarray(L0_ffn_shared_up, np.float32),
                    np.asarray(L0_ffn_shared_down, np.float32))
    if "nc" not in _CACHED:
        nc, counts, _ = build_program(nt=8, debug=False)
        _CACHED["nc"] = nc
    nc = _CACHED["nc"]
    in_maps = []
    for b in range(8):
        m = dict(w)
        m["xT"] = np.ascontiguousarray(x[b].T)
        in_maps.append(m)
    res = run_bass_kernel_spmd(nc, in_maps, core_ids=list(range(8)))
    out = np.stack([np.ascontiguousarray(r["outT"].T) for r in res.results], axis=0)
    return out
